# version 10
# baseline (speedup 1.0000x reference)
import contextlib
import math
import numpy as np
import ml_dtypes

import concourse.bass as bass
import concourse.mybir as mybir
from concourse.bass_utils import run_bass_kernel_spmd

F32 = mybir.dt.float32
BF16 = mybir.dt.bfloat16
I32 = mybir.dt.int32
AF = mybir.ActivationFunctionType
ALU = mybir.AluOpType
AX = mybir.AxisListType
BF16NP = ml_dtypes.bfloat16

NC = 8
D = 2048
TOWN = 1024
TB = 2048
HD = 128
NHMY = 8
RS_SUB = 256
RS_CALLS = 4

PAIRS = [[0, 1], [2, 3], [4, 5], [6, 7]]
ALL8 = [list(range(NC))]
MASK_NEG = -10000.0
ISQH = 1.0 / math.sqrt(HD)


class _SB:

    def __init__(self, ap2d, prog, off, nbytes):
        self._ap = ap2d
        self._prog = prog
        self._off = off
        self._nbytes = nbytes

    def __getitem__(self, k):
        return self._ap[k]

    def free(self):
        if self._nbytes:
            self._prog._arena_free(self._off, self._nbytes)
            self._prog.sbuf_live -= self._nbytes
            self._nbytes = 0


class Prog:
    ARENA_BYTES = 198 * 1024

    def __init__(self):
        self.nc = bass.Bass("TRN2", target_bir_lowering=False)
        self.body = {e: [] for e in "vstgy"}
        self._cms = []
        self.sems = {}
        self.cnt = {}
        self.sbuf_live = 0
        self.sbuf_peak = 0
        self.n_cc = 0
        self.tp_guard = 0

    def sem(self, name):
        if name not in self.sems:
            cm = self.nc.semaphore(name)
            self.sems[name] = cm.__enter__()
            self._cms.append(cm)
            self.cnt[name] = 0
        return self.sems[name]

    def _arena_t(self):
        if not hasattr(self, "_arena"):
            cm = self.nc.sbuf_tensor("arena", [128, self.ARENA_BYTES], mybir.dt.uint8)
            self._arena = cm.__enter__()
            self._cms.append(cm)
            self._free = [(0, self.ARENA_BYTES)]
        return self._arena

    def _arena_alloc(self, nbytes):
        nbytes = (nbytes + 31) // 32 * 32
        for idx, (off, ln) in enumerate(self._free):
            if ln >= nbytes:
                if ln == nbytes:
                    self._free.pop(idx)
                else:
                    self._free[idx] = (off + nbytes, ln - nbytes)
                return off, nbytes
        raise RuntimeError(f"arena OOM: need {nbytes}, free={self._free}")

    def _arena_free(self, off, nbytes):
        self._free.append((off, nbytes))
        self._free.sort()
        merged = []
        for o, l in self._free:
            if merged and merged[-1][0] + merged[-1][1] == o:
                merged[-1] = (merged[-1][0], merged[-1][1] + l)
            else:
                merged.append([o, l])
        self._free = [tuple(x) for x in merged]

    def sbuf(self, name, shape, dt):
        assert len(shape) == 2 and shape[0] <= 128
        t = self._arena_t()
        esz = mybir.dt.size(dt)
        off, nbytes = self._arena_alloc(int(shape[1]) * esz)
        ap = t[:, off:off + int(shape[1]) * esz].bitcast(dt)
        if shape[0] != 128:
            ap = ap[0:shape[0], :]
        self.sbuf_live += nbytes
        self.sbuf_peak = max(self.sbuf_peak, self.sbuf_live)
        return _SB(ap, self, off, nbytes)

    def psum(self, name, shape, dt):
        cm = self.nc.psum_tensor(name, list(shape), dt)
        t = cm.__enter__()
        self._cms.append(cm)
        return t

    def dram(self, name, shape, dt, kind=None, shared=False):
        if kind:
            return self.nc.dram_tensor(name, list(shape), dt, kind=kind).ap()
        if shared:
            return self.nc.dram_tensor(name, list(shape), dt, addr_space="Shared").ap()
        return self.nc.dram_tensor(name, list(shape), dt).ap()

    def emit(self, e, fn):
        self.body[e].append(fn)

    def _waits(self, eng, waits):
        merged = {}
        for s, v in waits:
            merged[s] = max(merged.get(s, 0), v)
        for s, v in merged.items():
            if v > 0:
                eng.wait_ge(self.sems[s], v)

    def op(self, e, name, waits=(), **kw):
        assert e in "vsg"
        self.sem(e)
        prev = self.cnt[e]
        self.cnt[e] += 1
        waits = tuple(waits)

        def fn(eng, prev=prev, waits=waits, name=name, kw=kw, e=e):
            self._waits(eng, waits)
            if prev > 0:
                eng.wait_ge(self.sems[e], prev)
            getattr(eng, name)(**kw).then_inc(self.sems[e], 1)
        self.emit(e, fn)
        return self.cnt[e]

    def vop(self, name, waits=(), **kw):
        return self.op("v", name, waits, **kw)

    def sop(self, name, waits=(), **kw):
        return self.op("s", name, waits, **kw)

    def pe(self, name, waits=(), inc=False, **kw):
        self.sem("t")
        if inc:
            self.cnt["t"] += 1

        def fn(eng, waits=tuple(waits), name=name, kw=kw, inc=inc):
            self._waits(eng, waits)
            inst = getattr(eng, name)(**kw)
            if inc:
                inst.then_inc(self.sems["t"], 1)
        self.emit("t", fn)
        return self.cnt["t"]

    def pe_wait(self, waits):
        if not waits:
            return

        def fn(eng, waits=tuple(waits)):
            self._waits(eng, waits)
        self.emit("t", fn)

    def wait_on(self, e, waits):
        def fn(eng, waits=tuple(waits)):
            self._waits(eng, waits)
        self.emit(e, fn)

    def dma(self, q, out, in_, waits=(), sem=None, tp=False):
        sname = sem or ("d" + q)
        self.sem(sname)
        self.cnt[sname] += 16
        mark = self.cnt[sname]

        def fn(eng, waits=tuple(waits), out=out, in_=in_, sname=sname, tp=tp):
            self._waits(eng, waits)
            if tp:
                eng.dma_start(out=out, in_=in_, transpose=True).then_inc(self.sems[sname], 16)
            else:
                eng.dma_start(out=out, in_=in_).then_inc(self.sems[sname], 16)
        self.emit(q, fn)
        return (sname, mark)

    def fence(self, extra=()):
        return [("v", self.cnt.get("v", 0)), ("s", self.cnt.get("s", 0)),
                ("t", self.cnt.get("t", 0))] + list(extra)

    def coll(self, kind, op, groups, ins, outs, waits=()):
        name = f"cc{self.n_cc}"
        self.n_cc += 1
        self.sem(name)

        def fn(eng, waits=tuple(waits), ins=ins, outs=outs, kind=kind, op=op,
               groups=groups, name=name):
            self._waits(eng, waits)
            eng.collective_compute(kind, op, replica_groups=groups,
                                   ins=ins, outs=outs).then_inc(self.sems[name])
        self.emit("g", fn)
        return (name, 1)

    def build(self):
        with contextlib.ExitStack() as stk:
            block = stk.enter_context(self.nc.Block())
            decs = {"v": block.vector, "s": block.scalar, "t": block.tensor,
                    "g": block.gpsimd, "y": block.sync}
            for e in "vstgy":
                ops = list(self.body[e])

                def runner(eng, ops=ops):
                    for fn in ops:
                        fn(eng)
                decs[e](runner)
        for cm in reversed(self._cms):
            cm.__exit__(None, None, None)
        return self.nc


class Banks:
    def __init__(self, P, psums):
        self.P = P
        self.ps = psums
        self.busy = [None] * len(psums)
        self.rr = 0

    def acquire(self):
        i = self.rr % len(self.ps)
        self.rr += 1
        if self.busy[i] is not None:
            self.P.pe_wait([self.busy[i]])
            self.busy[i] = None
        return i

    def release(self, i, eng_mark):
        self.busy[i] = eng_mark


def chain(P, banks, steps, n=512):
    i = banks.acquire()
    ps = banks.ps[i]
    last = len(steps) - 1
    for k, (l, r) in enumerate(steps):
        P.pe("matmul", out=ps[:, 0:n], lhsT=l, rhs=r,
             start=(k == 0), stop=(k == last), inc=(k == last))
    return ps, i, P.cnt["t"]


def r3(ap2d, inner):
    return ap2d.rearrange("p (a b) -> p a b", b=inner)


def tile_rows(dram_ap, p=128):
    return dram_ap.rearrange("(a p) c -> p a c", p=p)


def bcast_free(ap2d_slice, n):
    apdims = [list(d) for d in ap2d_slice.ap]
    return bass.AP(tensor=ap2d_slice.tensor, offset=ap2d_slice.offset,
                   ap=apdims + [[0, n]])


def emit_layernorm(P, src_ap, dst_bf, ntiles, tmp, marks_in=()):
    out_marks = []
    sq, st, eps = tmp["sq"], tmp["st"], tmp["eps"]
    for i in range(ntiles):
        x = src_ap(i)
        w = tuple(marks_in) if i == 0 else ()
        m_sq = P.sop("activation", waits=w, out=sq[:, 0:D], in_=x, func=AF.Square)
        P.vop("tensor_reduce", waits=w, out=st[:, 0:1], in_=x, axis=AX.X, op=ALU.add)
        P.vop("tensor_reduce", waits=[("s", m_sq)], out=st[:, 1:2], in_=sq[:, 0:D], axis=AX.X, op=ALU.add)
        P.vop("tensor_scalar", out=st[:, 0:1], in0=st[:, 0:1], scalar1=1.0 / D, scalar2=None, op0=ALU.mult)
        P.vop("tensor_scalar", out=st[:, 1:2], in0=st[:, 1:2], scalar1=1.0 / D, scalar2=None, op0=ALU.mult)
        P.vop("tensor_tensor", out=st[:, 2:3], in0=st[:, 0:1], in1=st[:, 0:1], op=ALU.mult)
        m_v = P.vop("tensor_tensor", out=st[:, 1:2], in0=st[:, 1:2], in1=st[:, 2:3], op=ALU.subtract)
        P.sop("activation", waits=[("v", m_v)], out=st[:, 2:3], in_=st[:, 1:2], func=AF.Ln, bias=eps[:, 0:1])
        m_r = P.sop("activation", out=st[:, 3:4], in_=st[:, 2:3], func=AF.Exp, scale=-0.5)
        m = P.vop("tensor_scalar", waits=[("s", m_r)], out=dst_bf(i), in0=x,
                  scalar1=st[:, 0:1], scalar2=st[:, 3:4], op0=ALU.subtract, op1=ALU.mult)
        out_marks.append(m)
    return out_marks


def emit_transpose_to(P, src_tile_fn, dst_fn, ngroups, ident, ps_tp, waits_first=(),
                      gsz=8):
    for g in range(ngroups):
        w = list(waits_first) if g == 0 else []
        if P.tp_guard:
            w.append(("v", P.tp_guard))
        P.pe_wait(w)
        for j in range(gsz):
            P.pe("transpose", out=ps_tp[:, j * 128:(j + 1) * 128],
                 in_=src_tile_fn(g, j), identity=ident, inc=(j == gsz - 1))
        mk = P.cnt["t"]
        P.tp_guard = P.vop("tensor_copy", waits=[("t", mk)],
                           out=dst_fn(g), in_=r3(ps_tp[:, 0:gsz * 128], 128))
    return P.tp_guard


def _dbg_finish(P, io):
    y0 = io["y0"]
    t = P.sbuf("dbg_z", [128, D], F32)
    allw = [(s, c) for s, c in P.cnt.items() if c > 0]
    P.wait_on("g", allw)
    m = P.op("g", "memset", ap=t[:, :], constant=0.0)
    P.wait_on("g", [("v", P.cnt["v"]), ("s", P.cnt["s"]), ("t", P.cnt["t"])])
    mm = None
    for i in range(8):
        mm = P.dma("g", out=y0[i * 128:(i + 1) * 128, :], in_=t[:, :],
                   waits=[("g", m)] if i == 0 else (), sem="dout")
    P.wait_on("g", [mm] + [(f"cc{i}", 1) for i in range(P.n_cc)])



def emit_l0(P, io):
    d_xnT_in = P.dram("d_xnT_in", [D, TOWN], BF16)
    d_xnT = P.dram("d_xnT", [NC * D, TOWN], BF16)
    d_wdq = P.dram("d_wdq", [48 * 128, D], BF16)
    d_at0_in = P.dram("d_at0_in", [TB, D], BF16)
    d_at0 = P.dram("d_at0", [TOWN, D], BF16)
    d_part = P.dram("d_part", [RS_CALLS * NC * RS_SUB, D], F32)
    d_ffn = P.dram("d_ffn", [TOWN, D], F32)
    io.update(d_ffn=d_ffn, d_at0=d_at0, d_xnT=d_xnT)

    xb, xo = io["xb"], io["xo"]

    sb_ident = P.sbuf("c_ident", [128, 128], BF16)
    sb_mask = P.sbuf("c_mask", [128, 128], F32)
    sb_ones = P.sbuf("c_ones", [128, 128], BF16)
    sb_cos = P.sbuf("c_cos", [128, 16 * 64], F32)
    sb_sin = P.sbuf("c_sin", [128, 16 * 64], F32)
    sb_eps = P.sbuf("c_eps", [128, 1], F32)
    sb_m16 = P.sbuf("c_m16", [128, 1], F32)
    sb_b4 = P.sbuf("c_b4", [128, 1], F32)
    sb_b6 = P.sbuf("c_b6", [128, 1], F32)
    sb_lnsq = P.sbuf("ln_sq", [128, D], F32)
    sb_lnst = P.sbuf("ln_st", [128, 8], F32)
    ln_tmp = dict(sq=sb_lnsq, st=sb_lnst, eps=sb_eps)
    io["sb_ident"] = sb_ident
    io["sb_ones"] = sb_ones
    io["sb_m16"] = sb_m16

    P.op("g", "memset", ap=sb_eps[:, :], constant=1e-5)
    P.op("g", "memset", ap=sb_m16[:, :], constant=-16.0)
    P.op("g", "memset", ap=sb_b4[:, :], constant=-4.0)
    m_const = P.op("g", "memset", ap=sb_b6[:, :], constant=-6.0)
    io["m_const"] = m_const

    P.dma("y", out=sb_ident[:, :], in_=io["c_ident_in"][:, :], sem="dcst")
    P.dma("y", out=sb_mask[:, :], in_=io["c_mask_in"][:, :], sem="dcst")
    P.dma("y", out=sb_ones[:, :], in_=io["c_ones_in"][:, :], sem="dcst")
    P.dma("y", out=r3(sb_cos[:, :], 64), in_=tile_rows(io["rope_cos"][:, :]), sem="dcst")
    P.dma("y", out=r3(sb_sin[:, :], 64), in_=tile_rows(io["rope_sin"][:, :]), sem="dcst")
    w_cst = ("dcst", P.cnt["dcst"])
    io["w_cst"] = w_cst

    ps = [P.psum(f"ps{i}", [128, 512], F32) for i in range(7)]
    ps_tp = P.psum("ps_tp", [128, 1024], BF16)
    banks = Banks(P, ps[0:5])
    io.update(ps=ps, ps_tp=ps_tp, banks=banks)

    sb_xt = P.sbuf("lnx", [128, 2 * D], F32)
    sb_xnt = P.sbuf("lnxn", [128, D], BF16)
    xt3 = r3(sb_xt[:, :], D)
    ln_state = {"prev": [0, 0]}

    def stream_ln_t(src_dram, ntiles, dst3_fn, first_waits=(), sems=("dxa", "dxb")):
        for i in range(ntiles):
            buf = i % 2
            w = list(first_waits) if i == 0 else []
            if ln_state["prev"][buf]:
                w.append(("v", ln_state["prev"][buf]))
            m_ld = P.dma("y", out=xt3[:, buf, :], in_=src_dram[i * 128:(i + 1) * 128, :],
                         waits=w, sem=sems[buf])
            if i > 0:
                P.wait_on("v", [("t", P.cnt["t"])])
            lnm = emit_layernorm(P, lambda _: xt3[:, buf, :], lambda _: sb_xnt[:, 0:D], 1,
                                 ln_tmp, marks_in=[m_ld])[0]
            ln_state["prev"][buf] = lnm
            emit_transpose_to(
                P, lambda g, j: sb_xnt[:, (g * 8 + j) * 128:(g * 8 + j + 1) * 128],
                lambda g, i=i: dst3_fn(i, g),
                2, sb_ident[:, :], ps_tp, waits_first=[("v", lnm)])
        P.wait_on("v", [("t", P.cnt["t"])])

    sb_xnT = P.sbuf("sb_xnT", [128, 16 * TB], BF16)
    xnT3 = r3(sb_xnT[:, :], TB)
    stream_ln_t(xb[:, :], 16,
                lambda i, g: xnT3[:, g * 8:(g + 1) * 8, i * 128:(i + 1) * 128],
                first_waits=[("g", m_const), w_cst])
    m_xnT = P.cnt["v"]

    sb_wq = P.sbuf("sb_wq", [128, 16 * 1024], BF16)
    sb_wkv = P.sbuf("sb_wkv", [128, 16 * 512], BF16)
    P.dma("g", out=r3(sb_wq[:, :], 1024), in_=tile_rows(io["wq"][:, :]), sem="dwq")
    P.dma("g", out=r3(sb_wkv[:, :], 512)[:, :, 0:256], in_=tile_rows(io["wk"][:, :]), sem="dwk")
    P.dma("g", out=r3(sb_wkv[:, :], 512)[:, :, 256:512], in_=tile_rows(io["wv"][:, :]), sem="dwk")

    sb_V = P.sbuf("sb_V", [128, 16 * 256], BF16)
    V3 = r3(sb_V[:, :], 256)
    wq3 = r3(sb_wq[:, :], 1024)
    wkv3 = r3(sb_wkv[:, :], 512)

    P.pe_wait([("v", m_xnT), ("dwq", 16), ("dwk", 32), w_cst])

    sb_Q = P.sbuf("sb_Q", [128, 16 * 1024], BF16)
    sb_K = P.sbuf("sb_K", [128, 16 * 256], BF16)
    Q3 = r3(sb_Q[:, :], 1024)
    K3 = r3(sb_K[:, :], 256)
    for tt in range(16):
        for dq in range(2):
            steps = [(xnT3[:, ht, tt * 128:(tt + 1) * 128],
                      wq3[:, ht, dq * 512:(dq + 1) * 512]) for ht in range(16)]
            p, i, mk = chain(P, banks, steps)
            m = P.vop("tensor_copy", waits=[("t", mk)],
                      out=Q3[:, tt, dq * 512:(dq + 1) * 512], in_=p[:, 0:512])
            banks.release(i, ("v", m))
        stepskv = [(xnT3[:, ht, tt * 128:(tt + 1) * 128], wkv3[:, ht, 0:512]) for ht in range(16)]
        p, i, mk = chain(P, banks, stepskv)
        m = P.vop("tensor_copy", waits=[("t", mk)], out=K3[:, tt, 0:256], in_=p[:, 0:256])
        m = P.vop("tensor_copy", out=V3[:, tt, :], in_=p[:, 256:512])
        banks.release(i, ("v", m))

    sb_xnT.free(); sb_wq.free(); sb_wkv.free()
    sb_QT = P.sbuf("sb_QT", [128, NHMY * TB], BF16)
    sb_KT = P.sbuf("sb_KT", [128, 2 * TB], BF16)
    QT3 = r3(sb_QT[:, :], TB)
    KT3 = r3(sb_KT[:, :], TB)

    sb_rp = P.sbuf("sb_rp", [128, 1024], F32)
    cosT3 = r3(sb_cos[:, :], 64)
    sinT3 = r3(sb_sin[:, :], 64)

    def rope_tm(t3, tt, nh):
        hd3 = t3[:, tt, :].rearrange("p (h d) -> p h d", d=128)
        lo = hd3[:, :, 0:64]
        hi = hd3[:, :, 64:128]
        rp4 = sb_rp[:, 0:nh * 128].rearrange("p (h d) -> p h d", d=128)
        ra = rp4[:, :, 0:64]
        rb = rp4[:, :, 64:128]
        cosb = bcast_free(cosT3[:, tt, :], nh).transpose([0, 2, 1])
        sinb = bcast_free(sinT3[:, tt, :], nh).transpose([0, 2, 1])
        P.vop("tensor_tensor", out=ra, in0=lo, in1=cosb, op=ALU.mult)
        P.vop("tensor_tensor", out=rb, in0=hi, in1=sinb, op=ALU.mult)
        P.vop("tensor_tensor", out=hi, in0=hi, in1=cosb, op=ALU.mult)
        P.vop("tensor_tensor", out=lo, in0=lo, in1=sinb, op=ALU.mult)
        P.vop("tensor_tensor", out=hi, in0=lo, in1=hi, op=ALU.add)
        P.vop("tensor_tensor", out=lo, in0=ra, in1=rb, op=ALU.subtract)

    for tt in range(16):
        rope_tm(Q3, tt, 8)
        rope_tm(K3, tt, 2)
    m_rope = P.cnt["v"]
    sb_rp.free()

    for tt in range(16):
        emit_transpose_to(
            P, lambda g, j, tt=tt: Q3[:, tt, (g * 8 + j) * 128:(g * 8 + j + 1) * 128],
            lambda g, tt=tt: QT3[:, g * 8:(g + 1) * 8, tt * 128:(tt + 1) * 128],
            1, sb_ident[:, :], ps_tp,
            waits_first=[("v", m_rope)] if tt == 0 else ())
    for tt in range(16):
        emit_transpose_to(
            P, lambda g, j, tt=tt: K3[:, tt, j * 128:(j + 1) * 128],
            lambda g, tt=tt: KT3[:, 0:2, tt * 128:(tt + 1) * 128],
            1, sb_ident[:, :], ps_tp, gsz=2)
    m_tQK = P.tp_guard
    sb_Q.free(); sb_K.free()

    sb_oT = P.sbuf("sb_oT", [128, 16 * TOWN], BF16)
    oT3 = r3(sb_oT[:, :], TOWN)
    stream_ln_t(xo[:, :], 8,
                lambda i, g: oT3[:, g * 8:(g + 1) * 8, i * 128:(i + 1) * 128],
                sems=("dxc", "dxd"))
    m_oT = P.cnt["v"]
    w_oT = P.dma("g", out=tile_rows(d_xnT_in[:, :]), in_=r3(sb_oT[:, :], TOWN),
                 waits=[("v", m_oT)], sem="dxo")
    w_ag1 = P.coll("AllGather", ALU.bypass, ALL8, [d_xnT_in[:, :].opt()], [d_xnT[:, :].opt()],
                   waits=[w_oT])
    io["w_ag1"] = w_ag1
    io.setdefault("reuse_guard", []).extend([w_oT])
    sb_oT.free(); sb_xt.free(); sb_xnt.free()

    deq = _DequantEmitter(P, io, d_wdq, sb_b4, sb_b6, sb_ident, ps_tp,
                          extra_first=io["reuse_guard"] + [("v", P.cnt["v"])])
    io["dq_done"] = deq.done

    sb_PT = P.sbuf("sb_PT", [128, 2 * 16 * 512], BF16)
    sb_bc = P.sbuf("sb_bc", [128, 2 * 512], F32)
    sb_denb = P.sbuf("sb_denb", [1, 2 * 512], BF16)
    sb_aT = P.sbuf("sb_aT", [128, NHMY * TB], BF16)
    aT3 = r3(sb_aT[:, :], TB)
    PT4 = sb_PT[:, :].rearrange("p (g k t) -> p g k t", g=2, k=16)
    bc3 = r3(sb_bc[:, :], 512)
    dnb3 = r3(sb_denb[:, :], 512)

    P.pe_wait([("v", m_tQK)] + io["reuse_guard"])
    blocks = [(hh, qc) for hh in range(NHMY) for qc in range(4)]
    pt_guard = [0, 0]
    pend = None

    def consume(st):
        hh, qc, kts, par, exp_last = st
        kvh = hh // 4
        P.pe_wait([("s", exp_last)])
        bi_pv = banks.acquire()
        pv = banks.ps[bi_pv]
        for k, kt in enumerate(kts):
            P.pe("matmul", out=pv[:, 0:512],
                 lhsT=V3[:, kt, kvh * 128:(kvh + 1) * 128], rhs=PT4[:, par, kt, :],
                 start=(k == 0), stop=(k == len(kts) - 1), inc=(k == len(kts) - 1))
        mk_pv = P.cnt["t"]
        bi_dn = banks.acquire()
        dn = banks.ps[bi_dn]
        for k, kt in enumerate(kts):
            P.pe("matmul", out=dn[0:1, 0:512],
                 lhsT=sb_ones[:, 0:1], rhs=PT4[:, par, kt, :],
                 start=(k == 0), stop=(k == len(kts) - 1), inc=(k == len(kts) - 1))
        mk_dn = P.cnt["t"]
        pt_guard[par] = mk_dn
        P.vop("reciprocal", waits=[("t", mk_dn)], out=bc3[0:1, par, :], in_=dn[0:1, 0:512])
        m_rc = P.vop("tensor_copy", out=dnb3[0:1, par, :], in_=bc3[0:1, par, :])
        banks.release(bi_dn, ("v", m_rc))
        P.pe_wait([("v", m_rc)])
        bi_bc = banks.acquire()
        bc = banks.ps[bi_bc]
        P.pe("matmul", out=bc[:, 0:512], lhsT=sb_ones[0:1, 0:128],
             rhs=dnb3[0:1, par, :], start=True, stop=True, inc=True)
        mk_bc = P.cnt["t"]
        m_bc = P.vop("tensor_copy", waits=[("t", mk_bc)], out=bc3[:, par, :], in_=bc[:, 0:512])
        banks.release(bi_bc, ("v", m_bc))
        m_at = P.vop("tensor_tensor", waits=[("t", mk_pv)],
                     out=aT3[:, hh, qc * 512:(qc + 1) * 512],
                     in0=pv[:, 0:512], in1=bc3[:, par, :], op=ALU.mult)
        banks.release(bi_pv, ("v", m_at))

    for idx, (hh, qc) in enumerate(blocks):
        par = idx % 2
        kvh = hh // 4
        kts = list(range(0, 4 * qc + 4))
        exp_last = 0
        for ki, kt in enumerate(kts):
            bi = banks.acquire()
            pss = banks.ps[bi]
            P.pe("matmul", out=pss[:, 0:512],
                 lhsT=KT3[:, kvh, kt * 128:(kt + 1) * 128],
                 rhs=QT3[:, hh, qc * 512:(qc + 1) * 512],
                 start=True, stop=True, inc=True)
            mk = P.cnt["t"]
            wv_ = [("t", mk)]
            if ki == 0 and pt_guard[par]:
                wv_.append(("t", pt_guard[par]))
            if qc * 512 <= kt * 128 < (qc + 1) * 512:
                off = kt * 128 - qc * 512
                if off > 0:
                    mm = P.vop("tensor_scalar", waits=wv_, out=pss[:, 0:off],
                               in0=pss[:, 0:off], scalar1=MASK_NEG, scalar2=None,
                               op0=ALU.add)
                    wv_ = [("v", mm)]
                mm = P.vop("tensor_tensor", waits=wv_, out=pss[:, off:off + 128],
                           in0=pss[:, off:off + 128], in1=sb_mask[:, :], op=ALU.add)
                wv_ = [("v", mm)]
            if ki == 0 and pt_guard[par]:
                wv_.append(("t", pt_guard[par]))
            exp_last = P.sop("activation", waits=wv_, out=PT4[:, par, kt, :],
                             in_=pss[:, 0:512],
                             func=AF.Exp, bias=sb_m16[:, 0:1], scale=ISQH)
            banks.release(bi, ("s", exp_last))
        st = (hh, qc, kts, par, exp_last)
        if pend is not None:
            consume(pend)
        pend = st
        deq.step()
        deq.step()
    consume(pend)

    while deq.step():
        pass
    sb_PT.free(); sb_bc.free(); sb_denb.free()
    sb_QT.free(); sb_KT.free()

    sb_wo = P.sbuf("sb_wo", [128, 8 * D], BF16)
    wo3 = r3(sb_wo[:, :], D)
    w_wo = P.dma("g", out=r3(sb_wo[:, :], D), in_=tile_rows(io["wo0"][:, :]), sem="dwo",
                 waits=P.fence())
    P.pe_wait([w_wo])
    sb_opst = P.sbuf("sb_opst", [128, D], BF16)
    w_at0 = None
    for tt in range(16):
        if w_at0:
            P.wait_on("v", [w_at0])
        hcm = 0
        for hc in range(4):
            steps = [(aT3[:, hh, tt * 128:(tt + 1) * 128],
                      wo3[:, hh, hc * 512:(hc + 1) * 512]) for hh in range(NHMY)]
            p, i, mk = chain(P, banks, steps)
            hcm = P.vop("tensor_copy", waits=[("t", mk)],
                        out=sb_opst[:, hc * 512:(hc + 1) * 512], in_=p[:, 0:512])
            banks.release(i, ("v", hcm))
        w_at0 = P.dma("y", out=d_at0_in[tt * 128:(tt + 1) * 128, :], in_=sb_opst[:, :],
                      waits=[("v", hcm)], sem="dat")
    w_rs_at0 = P.coll("ReduceScatter", ALU.add, PAIRS, [d_at0_in[:, :].opt()],
                      [d_at0[:, :].opt()], waits=[w_at0])
    io["w_rs_at0"] = w_rs_at0
    sb_opst.free(); sb_aT.free(); sb_V.free(); sb_wo.free()
    deq.free()
    dq_done = deq.done

    sb_cos.free(); sb_sin.free(); sb_lnsq.free(); sb_lnst.free(); sb_mask.free()

    if io.get("early_prep"):
        io["prep_first"] = [w_rs_at0, dq_done[47], io["w_ag1"]]
        emit_l1_prep(P, io, part="compute")

    sb_wd = P.sbuf("ffn_wd", [128, 16 * D], BF16)
    sb_rhs = P.sbuf("ffn_rhs", [128, 4 * 16 * RS_SUB], BF16)
    sb_inter = P.sbuf("ffn_inter", [128, 16 * NC * RS_SUB], BF16)
    sb_lw = P.sbuf("ffn_lw", [128, 4 * D], BF16)
    sb_gel = P.sbuf("ffn_gel", [128, RS_SUB], F32)
    sb_dstg = P.sbuf("ffn_dstg", [128, D], F32)
    rhs4 = sb_rhs[:, :].rearrange("p (r h t) -> p r h t", r=4, h=16)
    lw3 = r3(sb_lw[:, :], D)
    wd4 = sb_wd[:, :].rearrange("p (i s j) -> p i s j", i=16, s=16)
    int3 = sb_inter[:, :].rearrange("p (i rt) -> p i rt", i=16)

    w_wd = P.dma("g", out=r3(sb_wd[:, :], D), in_=tile_rows(d_wdq[32 * 128:48 * 128, :]),
                 waits=P.fence([dq_done[47]] + io.get("prep_guard", [])), sem="dwd")
    lw_busy = [None] * 4
    rs_waits = []
    w_part = None
    m_int_prev = 0
    for cj in range(RS_CALLS):
        for rh in range(2):
            w_rhs = None
            for rr in range(4):
                r = rh * 4 + rr
                src = d_xnT[r * D:(r + 1) * D, cj * RS_SUB:(cj + 1) * RS_SUB]
                wl = []
                if cj == 0 and rh == 0 and rr == 0:
                    wl = P.fence([io["w_ag1"], dq_done[47]] + io.get("prep_guard", []))
                if (rh + cj) and rr == 0:
                    wl.append(("t", P.cnt["t"]))
                w_rhs = P.dma("g", out=rhs4[:, rr, :, :], in_=tile_rows(src),
                              waits=wl, sem="drh")
            for i in range(16):
                bg, bu = (2 * i) % 4, (2 * i + 1) % 4
                wg = (P.fence([dq_done[31]] + io.get("prep_guard", []))
                      if (cj == 0 and rh == 0 and i == 0) else [])
                if lw_busy[bg] is not None:
                    wg.append(lw_busy[bg])
                w_lg = P.dma("y", out=lw3[:, bg, :], in_=d_wdq[i * 128:(i + 1) * 128, :],
                             waits=wg, sem=f"dlw{bg}")
                wu = [lw_busy[bu]] if lw_busy[bu] is not None else []
                w_lu = P.dma("y", out=lw3[:, bu, :], in_=d_wdq[(16 + i) * 128:(17 + i) * 128, :],
                             waits=wu, sem=f"dlw{bu}")
                for rr in range(4):
                    wpe = [w_lg, w_lu, w_rhs] if rr == 0 else []
                    P.pe_wait(wpe)
                    stepsg = [(lw3[:, bg, ht * 128:(ht + 1) * 128], rhs4[:, rr, ht, :])
                              for ht in range(16)]
                    pg, big, mkg = chain(P, banks, stepsg, n=RS_SUB)
                    stepsu = [(lw3[:, bu, ht * 128:(ht + 1) * 128], rhs4[:, rr, ht, :])
                              for ht in range(16)]
                    pu, biu, mku = chain(P, banks, stepsu, n=RS_SUB)
                    r = rh * 4 + rr
                    m_gel = P.sop("activation", waits=[("t", mkg), ("v", m_int_prev)],
                                  out=sb_gel[:, :],
                                  in_=pg[:, 0:RS_SUB], func=AF.Gelu_apprx_tanh)
                    m_int = P.vop("tensor_tensor", waits=[("s", m_gel), ("t", mku)],
                                  out=int3[:, i, r * RS_SUB:(r + 1) * RS_SUB],
                                  in0=sb_gel[:, :], in1=pu[:, 0:RS_SUB], op=ALU.mult)
                    m_int_prev = m_int
                    banks.release(big, ("s", m_gel))
                    banks.release(biu, ("v", m_int))
                lw_busy[bg] = ("t", P.cnt["t"])
                lw_busy[bu] = ("t", P.cnt["t"])
        m_ph1_v = P.cnt["v"]
        first = True
        for r in range(NC):
            for th in range(2):
                if w_part:
                    P.wait_on("v", [w_part])
                hcm = 0
                for hc in range(4):
                    wpe = []
                    if first:
                        wpe = [("v", m_ph1_v), w_wd]
                        first = False
                    P.pe_wait(wpe)
                    steps = [(int3[:, nt, r * RS_SUB + th * 128:r * RS_SUB + (th + 1) * 128],
                              wd4[:, hc * 4:(hc + 1) * 4, nt, :]) for nt in range(16)]
                    p, bi, mk = chain(P, banks, steps)
                    hcm = P.vop("tensor_copy", waits=[("t", mk)],
                                out=sb_dstg[:, hc * 512:(hc + 1) * 512], in_=p[:, 0:512])
                    banks.release(bi, ("v", hcm))
                row = cj * 2048 + r * RS_SUB + th * 128
                w_part = P.dma("y", out=d_part[row:row + 128, :], in_=sb_dstg[:, :],
                               waits=[("v", hcm)], sem="ddp")
        rs_waits.append(
            P.coll("ReduceScatter", ALU.add, ALL8,
                   [d_part[cj * 2048:(cj + 1) * 2048, :].opt()],
                   [d_ffn[cj * RS_SUB:(cj + 1) * RS_SUB, :].opt()],
                   waits=[w_part]))
        if io.get("early_prep"):
            if cj == 0:
                emit_l1_prep_ag(P, io, "ug")
            elif cj == 1:
                emit_l1_prep_ag(P, io, "d")
    io["w_rs_ffn"] = rs_waits
    for t in (sb_wd, sb_rhs, sb_inter, sb_lw, sb_gel, sb_dstg):
        t.free()

    sb_r1 = P.sbuf("res_x", [128, 8 * D], F32)
    sb_r2 = P.sbuf("res_at", [128, 8 * D], BF16)
    sb_r3 = P.sbuf("res_ffn", [128, 8 * D], F32)
    P.dma("y", out=r3(sb_r1[:, :], D), in_=tile_rows(xo[:, :]), sem="dres",
          waits=P.fence([w_part] + io.get("prep_guard", [])))
    P.dma("y", out=r3(sb_r2[:, :], D), in_=tile_rows(d_at0[:, :]),
          waits=[w_rs_at0] + io.get("prep_guard", []), sem="dres")
    P.dma("y", out=r3(sb_r3[:, :], D), in_=tile_rows(d_ffn[:, :]),
          waits=list(rs_waits), sem="dres")
    a1 = r3(sb_r1[:, :], D)
    a2 = r3(sb_r2[:, :], D)
    a3 = r3(sb_r3[:, :], D)
    x1_marks = []
    for i in range(8):
        w = [("dres", 48)] if i == 0 else []
        P.vop("tensor_tensor", waits=w, out=a1[:, i, :], in0=a1[:, i, :], in1=a3[:, i, :], op=ALU.add)
        m = P.vop("tensor_tensor", out=a1[:, i, :], in0=a1[:, i, :], in1=a2[:, i, :], op=ALU.add)
        x1_marks.append(m)
    io["sb_x1"] = sb_r1
    io["x1_marks"] = x1_marks
    io["res_free"] = (sb_r2, sb_r3)
    return io


class _DequantEmitter:

    def __init__(self, P, io, d_wdq, sb_b4, sb_b6, sb_ident, ps_tp, extra_first=()):
        self.P = P
        self.io = io
        self.d_wdq = d_wdq
        self.b4, self.b6 = sb_b4, sb_b6
        self.ident, self.ps_tp = sb_ident, ps_tp
        self.pk = P.sbuf("dq_pk", [128, 2 * 1024], I32)
        self.sc = P.sbuf("dq_sc", [128, 2 * 64], F32)
        self.mi = P.sbuf("dq_mi", [128, 2048], I32)
        self.s8 = P.sbuf("dq_s8", [128, 2048], I32)
        self.a1 = P.sbuf("dq_a1", [128, 2048], BF16)
        self.a2 = P.sbuf("dq_a2", [128, 2048], BF16)
        self.mfb = P.sbuf("dq_mf", [128, 2048], BF16)
        self.sfb = P.sbuf("dq_sf", [128, 2048], BF16)
        self.nat = P.sbuf("dq_nat", [128, D], BF16)
        self.stg = P.sbuf("dq_stg", [128, D], BF16)
        self.pk3 = r3(self.pk[:, :], 1024)
        self.mf = self.mfb[:, :]
        self.sf = self.sfb[:, :]
        self.i = 0
        self.pk_prev = [0, 0]
        self.w_stg = None
        self.w_sc_use = 0
        self.done = {}
        self.extra_first = list(extra_first)
        self.srcs = [(io["gpk"], io["gsc"]), (io["upk"], io["usc"]), (io["dpk"], io["dsc"])]

    def free(self):
        for t in (self.pk, self.sc, self.mi, self.s8, self.a1, self.a2,
                  self.mfb, self.sfb, self.nat, self.stg):
            t.free()

    def step(self):
        P = self.P
        if self.i >= 48:
            return False
        mi_, i = divmod(self.i, 16)
        pkap, scap = self.srcs[mi_]
        scb_ = self.i % 2
        w = [("v", self.w_sc_use)] if self.w_sc_use else list(self.extra_first)
        self.w_sc = P.dma("y", out=self.sc[:, scb_ * 64:(scb_ + 1) * 64],
                          in_=scap[i * 128:(i + 1) * 128, :], waits=w, sem=f"dsc{scb_}")
        buf = self.i % 2
        w = [("v", self.pk_prev[buf])] if self.pk_prev[buf] else list(self.extra_first)
        w_ld = P.dma("y", out=self.pk3[:, buf, :], in_=pkap[i * 128:(i + 1) * 128, :],
                     waits=w, sem=f"dpk{buf}")
        P.vop("tensor_scalar", waits=[w_ld, self.w_sc], out=self.mi[:, 0:1024],
              in0=self.pk3[:, buf, :], scalar1=7, scalar2=None, op0=ALU.bitwise_and)
        P.vop("tensor_scalar", out=self.mi[:, 1024:2048], in0=self.pk3[:, buf, :],
              scalar1=4, scalar2=7, op0=ALU.logical_shift_right, op1=ALU.bitwise_and)
        P.vop("tensor_scalar", out=self.s8[:, 0:1024], in0=self.pk3[:, buf, :],
              scalar1=8, scalar2=None, op0=ALU.bitwise_and)
        self.pk_prev[buf] = P.vop(
            "tensor_scalar", out=self.s8[:, 1024:2048], in0=self.pk3[:, buf, :],
            scalar1=4, scalar2=8, op0=ALU.logical_shift_right, op1=ALU.bitwise_and)
        P.vop("tensor_copy", out=self.mf, in_=self.mi[:, :])
        m_cv = P.vop("tensor_copy", out=self.sf, in_=self.s8[:, :])
        P.sop("activation", waits=[("v", m_cv)], out=self.a1[:, :], in_=self.mf,
              func=AF.Relu, bias=self.b4[:, 0:1])
        m_a2 = P.sop("activation", out=self.a2[:, :], in_=self.mf,
                     func=AF.Relu, bias=self.b6[:, 0:1])
        P.vop("tensor_scalar", out=self.sf, in0=self.sf,
              scalar1=-0.25, scalar2=1.0, op0=ALU.mult, op1=ALU.add)
        P.vop("scalar_tensor_tensor", waits=[("s", m_a2)], out=self.a2[:, :],
              in0=self.mf, scalar=0.5, in1=self.a2[:, :], op0=ALU.mult, op1=ALU.add)
        P.vop("scalar_tensor_tensor", out=self.a2[:, :], in0=self.a1[:, :],
              scalar=0.5, in1=self.a2[:, :], op0=ALU.mult, op1=ALU.add)
        P.vop("tensor_tensor", out=self.a2[:, :], in0=self.a2[:, :], in1=self.sf,
              op=ALU.mult)
        natw = r3(self.nat[:, :], 2)
        v03 = r3(self.a2[:, :], 1024)
        for q in range(2):
            scb = bcast_free(self.sc[:, scb_ * 64:(scb_ + 1) * 64], 16)
            P.vop("tensor_tensor", out=natw[:, :, q], in0=v03[:, q, :],
                  in1=scb, op=ALU.mult)
        self.w_sc_use = P.cnt["v"]
        m_nat = P.cnt["v"]
        wtp = [("v", m_nat)]
        if self.w_stg:
            wtp.append(self.w_stg)
        emit_transpose_to(
            P, lambda g, j: self.nat[:, (g * 8 + j) * 128:(g * 8 + j + 1) * 128],
            lambda g: r3(self.stg[:, :], 128)[:, g * 8:(g + 1) * 8, :],
            2, self.ident[:, :], self.ps_tp, waits_first=wtp)
        blk = self.i
        self.w_stg = P.dma("y", out=self.d_wdq[blk * 128:(blk + 1) * 128, :],
                           in_=self.stg[:, :], waits=[("v", P.tp_guard)], sem="dst")
        P.wait_on("v", [("t", P.cnt["t"])])
        self.done[blk] = self.w_stg
        self.i += 1
        return True


def finish_l0_only(P, io, y0):
    sb_x1 = io["sb_x1"]
    w = None
    for i in range(8):
        w = P.dma("y", out=y0[i * 128:(i + 1) * 128, :], in_=r3(sb_x1[:, :], D)[:, i, :],
                  waits=[("v", io["x1_marks"][-1])] if i == 0 else (), sem="dout")
    P.wait_on("y", [w] + [(f"cc{i}", 1) for i in range(P.n_cc)])


N1 = 8192
SC_A = 1.0 / (64.0 * math.sqrt(2048.0))
SC_UP = 1.0 / math.sqrt(8192.0)
SC_OUT = 64.0


def emit_l1_prep(P, io, part="both"):
    if part in ("both", "ug", "compute"):
        d_w1ug_in = P.dram("d_w1ug_in", [D, D], BF16)
        d_w1ug = P.dram("d_w1ug", [NC * D, D], BF16)
        io.update(d_w1ug=d_w1ug, d_w1ug_in=d_w1ug_in)
        _prep_ug(P, io, d_w1ug_in, d_w1ug, ag=(part != "compute"))
    if part in ("both", "d", "compute"):
        d_w1d_in = P.dram("d_w1d_in", [256, N1], BF16)
        d_w1d = P.dram("d_w1d", [D, N1], BF16)
        io.update(d_w1d=d_w1d, d_w1d_in=d_w1d_in)
        _prep_d(P, io, d_w1d_in, d_w1d, ag=(part != "compute"))


def emit_l1_prep_ag(P, io, which):
    if which == "ug":
        io["w_ag_ug"] = P.coll("AllGather", ALU.bypass, ALL8,
                               [io["d_w1ug_in"][:, :].opt()],
                               [io["d_w1ug"][:, :].opt()], waits=[io["w_s_ug"]])
    else:
        io["w_ag_d"] = P.coll("AllGather", ALU.bypass, ALL8,
                              [io["d_w1d_in"][:, :].opt()],
                              [io["d_w1d"][:, :].opt()], waits=[io["w_s2_d"]])


def _prep_ug(P, io, d_w1ug_in, d_w1ug, ag=True):
    sb_t = P.sbuf("l1p_t", [128, 16 * D], BF16)
    w_c = P.dma("g", out=r3(sb_t[:, :], D), in_=tile_rows(io["w1ugt"][:, :]),
                sem="dl1a", waits=P.fence(io.get("prep_first", [])))
    w_s = P.dma("g", out=tile_rows(d_w1ug_in[:, :]), in_=r3(sb_t[:, :], D),
                waits=[w_c], sem="dl1a")
    io["w_s_ug"] = w_s
    if ag:
        emit_l1_prep_ag(P, io, "ug")
    sb_t.free()
    io.setdefault("prep_guard", []).extend([w_c, w_s])


def _prep_d(P, io, d_w1d_in, d_w1d, ag=True):
    sb_d = P.sbuf("l1p_d", [128, 2 * N1], F32)
    sb_db = P.sbuf("l1p_db", [128, 2 * N1], BF16)
    sb_sc = P.sbuf("l1p_sc", [128, 4096], F32)
    w_l = P.dma("y", out=r3(sb_d[:, :], N1), in_=tile_rows(io["w1d"][:, :]),
                sem="dl1c",
                waits=P.fence(io.get("prep_guard", []) + io.get("prep_first", [])))
    d3 = r3(sb_d[:, :], N1)
    first = [w_l]
    for lvl in range(6):
        S = 128 * (1 << lvl)
        for ti in range(2):
            tv = d3[:, ti, :].rearrange("p (blk c) -> p blk c", c=2 * S)
            a = tv[:, :, 0:S]
            b = tv[:, :, S:2 * S]
            nb = N1 // (2 * S)
            scr = r3(sb_sc[:, 0:nb * S], S)
            P.vop("tensor_tensor", waits=first, out=scr, in0=a, in1=b, op=ALU.subtract)
            first = []
            P.vop("tensor_tensor", out=a, in0=a, in1=b, op=ALU.add)
            P.vop("tensor_copy", out=b, in_=scr)
    m_bf = P.vop("tensor_copy", out=sb_db[:, :], in_=sb_d[:, :])
    w_s2 = P.dma("g", out=tile_rows(d_w1d_in[:, :]), in_=r3(sb_db[:, :], N1),
                 waits=[("v", m_bf)], sem="dl1c2")
    io["w_s2_d"] = w_s2
    if ag:
        emit_l1_prep_ag(P, io, "d")
    sb_d.free(); sb_db.free(); sb_sc.free()
    io.setdefault("prep_guard", []).extend([w_l, w_s2])


def emit_l1(P, io):
    d_xn1T_in = P.dram("d_xn1T_in", [D, TOWN], BF16)
    d_xn1T = P.dram("d_xn1T", [2 * D, TOWN], BF16)
    d_kxr = P.dram("d_kxr", [3 * D, TB], BF16)
    d_kvr = P.dram("d_kvr", [3 * 1024, TB], F32)
    d_at1_in = P.dram("d_at1_in", [TB, D], BF16)
    d_at1 = P.dram("d_at1", [TOWN, D], BF16)

    ps = io["ps"]
    ps_tp = io["ps_tp"]
    banks = io["banks"]
    sb_ident = io["sb_ident"]
    ident = sb_ident[:, :]

    sb_h128 = P.sbuf("c_h128", [128, 128], BF16)
    w_h = P.dma("y", out=sb_h128[:, :], in_=io["c_h128_in"][:, :], sem="dcl1",
                waits=P.fence(io.get("prep_guard", [])))
    sb_vec = P.sbuf("l1_vec", [128, 8 * 16], F32)
    vec3 = r3(sb_vec[:, :], 16)
    P.dma("y", out=vec3[:, 0, 0:8], in_=io["td8"][:, :], sem="dcl1")
    P.dma("y", out=vec3[:, 1, 0:8], in_=io["tf8"][:, :], sem="dcl1")
    P.dma("y", out=vec3[:, 2, :], in_=io["tmk16"][:, :], sem="dcl1")
    P.dma("y", out=vec3[:, 3, :], in_=io["tmv16"][:, :], sem="dcl1")
    P.dma("y", out=vec3[:, 4, :], in_=io["tmr16"][:, :], sem="dcl1")
    P.dma("y", out=vec3[:, 5, :], in_=io["suug16"][:, :], sem="dcl1")
    w_vec = ("dcl1", P.cnt["dcl1"])
    w_h = w_vec
    m_e1 = P.sop("activation", waits=[w_vec], out=vec3[:, 6, 0:8], in_=vec3[:, 0, 0:8], func=AF.Exp)
    m_eW = P.sop("activation", out=vec3[:, 6, 0:8], in_=vec3[:, 6, 0:8], func=AF.Exp, scale=-1.0)
    m_eU = P.sop("activation", out=vec3[:, 7, 0:8], in_=vec3[:, 1, 0:8], func=AF.Exp)
    m_sua = P.vop("tensor_scalar", waits=[w_vec], out=vec3[:, 5, :], in0=vec3[:, 5, :],
                  scalar1=SC_A, scalar2=None, op0=ALU.mult)

    sb_lnsq = P.sbuf("l1ln_sq", [128, D], F32)
    sb_lnst = P.sbuf("l1ln_st", [128, 8], F32)
    sb_eps2 = P.sbuf("l1_eps", [128, 1], F32)
    m_eps = P.op("g", "memset", ap=sb_eps2[:, :], constant=1e-5)
    ln_tmp = dict(sq=sb_lnsq, st=sb_lnst, eps=sb_eps2)

    sb_tps = P.sbuf("l1_tps", [128, 2 * 8 * 128], BF16)
    tps4 = sb_tps[:, :].rearrange("p (b j c) -> p b j c", b=2, j=8)
    tps_w = [None, None]
    sb_x1t = P.sbuf("l1_x1t", [128, D], F32)
    sb_xnt1 = P.sbuf("l1_xnt", [128, D], BF16)

    if "sb_x1" in io:
        x1src = lambda i: r3(io["sb_x1"][:, :], D)[:, i, :]
        x1waits = [("v", io["x1_marks"][-1])]
        ld = None
    else:
        x1d = io["x1"]
        ld = lambda i, w: P.dma("y", out=sb_x1t[:, :], in_=x1d[i * 128:(i + 1) * 128, :],
                                waits=w, sem="dx1t")
        x1src = lambda i: sb_x1t[:, 0:D]
        x1waits = []

    for i in range(8):
        w = list(x1waits) + ([("g", m_eps), ("s", m_eU)] if i == 0 else [])
        if ld is not None:
            prevln = [("v", P.cnt["v"]), ("s", P.cnt["s"])] if i > 0 else []
            wld = ld(i, w + prevln)
            w = [wld]
        lnm = emit_layernorm(P, lambda _: x1src(i), lambda _: sb_xnt1[:, 0:D], 1,
                             ln_tmp, marks_in=w)[0]
        for g in range(2):
            buf = (i * 2 + g) % 2
            wt = [("v", lnm)] if g == 0 else []
            if P.tp_guard:
                wt.append(("v", P.tp_guard))
            if tps_w[buf]:
                wt.append(tps_w[buf])
            P.pe_wait(wt)
            for j in range(8):
                P.pe("transpose", out=ps_tp[:, j * 128:(j + 1) * 128],
                     in_=sb_xnt1[:, (g * 8 + j) * 128:(g * 8 + j + 1) * 128],
                     identity=ident, inc=(j == 7))
            mk = P.cnt["t"]
            P.tp_guard = P.vop("tensor_copy", waits=[("t", mk)],
                               out=tps4[:, buf], in_=r3(ps_tp[:, 0:1024], 128))
            tps_w[buf] = P.dma(
                "g", out=tile_rows(d_xn1T_in[g * 1024:(g + 1) * 1024, i * 128:(i + 1) * 128]),
                in_=tps4[:, buf], waits=[("v", P.tp_guard)], sem=f"dxnt{buf}")
    if "sb_x1" in io:
        io["d_x1"] = P.dram("d_x1", [TOWN, D], F32)
        w_x1st = P.dma("g", out=tile_rows(io["d_x1"][:, :]), in_=r3(io["sb_x1"][:, :], D),
                       waits=[("v", io["x1_marks"][-1])], sem="dx1st")
        io["w_x1st"] = w_x1st
    w_agx = P.coll("AllGather", ALU.bypass, PAIRS, [d_xn1T_in[:, :].opt()],
                   [d_xn1T[:, :].opt()], waits=[tps_w[0], tps_w[1]])
    sb_x1t.free(); sb_xnt1.free(); sb_tps.free()
    sb_lnsq.free(); sb_lnst.free(); sb_eps2.free()
    if "sb_x1" in io:
        io["sb_x1"].free()
        io["x1_guard"] = [io["w_x1st"]]

    sb_full = P.sbuf("l1_full", [128, 16 * TB], BF16)
    fu3 = r3(sb_full[:, :], TB)
    w_f1 = P.dma("y", out=fu3[:, :, 0:TOWN], in_=tile_rows(d_xn1T[0:D, :]),
                 waits=P.fence([w_agx] + io.get("x1_guard", [])), sem="dfu")
    w_f2 = P.dma("y", out=fu3[:, :, TOWN:TB], in_=tile_rows(d_xn1T[D:2 * D, :]),
                 waits=[], sem="dfu")
    sb_dif = P.sbuf("l1_dif", [128, TB], BF16)
    sb_kx = P.sbuf("l1_kx", [128, 2 * TB], BF16)
    kx3 = r3(sb_kx[:, :], TB)
    kx_prev = [None, None]
    w_kxr = None
    first_w = [w_f1, w_f2]
    for ht in range(16):
        m_df = P.vop("tensor_tensor", waits=first_w,
                     out=sb_dif[:, 1:TB], in0=fu3[:, ht, 1:TB], in1=fu3[:, ht, 0:TB - 1],
                     op=ALU.subtract)
        first_w = []
        for pi, slot in enumerate((2, 3, 4)):
            buf = (ht * 3 + pi) % 2
            w0 = [kx_prev[buf]] if kx_prev[buf] else []
            P.vop("scalar_tensor_tensor", waits=w0,
                  out=kx3[:, buf, 1:TB], in0=sb_dif[:, 1:TB],
                  scalar=vec3[:, slot, ht:ht + 1], in1=fu3[:, ht, 0:TB - 1],
                  op0=ALU.mult, op1=ALU.add)
            m_kx = P.vop("tensor_scalar", out=kx3[:, buf, 0:1], in0=fu3[:, ht, 0:1],
                         scalar1=vec3[:, slot, ht:ht + 1], scalar2=None, op0=ALU.mult)
            w_kxr = P.dma("y", out=d_kxr[pi * D + ht * 128:pi * D + (ht + 1) * 128, :],
                          in_=kx3[:, buf, :], waits=[("v", m_kx)], sem=f"dkxr{buf}")
            kx_prev[buf] = w_kxr
    sb_full.free(); sb_dif.free(); sb_kx.free()

    sb_k = P.sbuf("l1s_k", [128, TB], F32)
    sb_v = P.sbuf("l1s_v", [128, TB], F32)
    sb_r = P.sbuf("l1s_r", [128, TB], F32)
    sb_E = P.sbuf("l1s_E", [128, TB], F32)
    sb_F = P.sbuf("l1s_F", [128, TB], F32)
    sb_Ev = P.sbuf("l1s_Ev", [128, TB], F32)
    sb_S1 = P.sbuf("l1s_S1", [128, TB], F32)
    sb_S2 = P.sbuf("l1s_S2", [128, TB], F32)
    sb_A = P.sbuf("l1s_A", [128, TB], F32)
    sb_m = P.sbuf("l1s_m", [128, 2], F32)
    sb_rrT = P.sbuf("l1_rrT", [128, 8 * TB], BF16)
    rr3 = r3(sb_rrT[:, :], TB)
    wk_prev = []
    m_rr = 0

    def emit_scan(dt, w_pst_dt):
        nonlocal wk_prev, m_rr
        w_k = P.dma("y", out=sb_k[:, :], in_=d_kvr[dt * 128:(dt + 1) * 128, :],
                    waits=[w_pst_dt] + wk_prev, sem="dsk")
        w_v = P.dma("y", out=sb_v[:, :], in_=d_kvr[1024 + dt * 128:1024 + (dt + 1) * 128, :],
                    sem="dsk")
        w_r = P.dma("y", out=sb_r[:, :], in_=d_kvr[2048 + dt * 128:2048 + (dt + 1) * 128, :],
                    sem="dsk")
        P.vop("tensor_reduce", waits=[w_k, w_v, w_r], out=sb_m[:, 0:1], in_=sb_k[:, :],
              axis=AX.X, op=ALU.max, negate=True)
        m_E = P.sop("activation", waits=[("v", P.cnt["v"])], out=sb_E[:, :], in_=sb_k[:, :],
                    func=AF.Exp, bias=sb_m[:, 0:1])
        P.vop("tensor_scalar", waits=[("s", m_E)], out=sb_F[:, :], in0=sb_E[:, :],
              scalar1=vec3[:, 7, dt:dt + 1], scalar2=None, op0=ALU.mult)
        P.vop("tensor_tensor", out=sb_Ev[:, :], in0=sb_E[:, :], in1=sb_v[:, :], op=ALU.mult)
        P.op("v", "memset", ap=sb_S1[:, 0:1], constant=0.0)
        P.op("v", "memset", ap=sb_S2[:, 0:1], constant=0.0)
        P.vop("tensor_copy", out=sb_S1[:, 1:TB], in_=sb_Ev[:, 0:TB - 1])
        P.vop("tensor_copy", out=sb_S2[:, 1:TB], in_=sb_E[:, 0:TB - 1])
        P.op("v", "memset", ap=sb_A[:, :], constant=1.0)
        P.vop("tensor_scalar", out=sb_A[:, :], in0=sb_A[:, :],
              scalar1=vec3[:, 6, dt:dt + 1], scalar2=None, op0=ALU.mult)
        P.vop("tensor_tensor_scan", out=sb_S1[:, :], data0=sb_A[:, :], data1=sb_S1[:, :],
              initial=0.0, op0=ALU.mult, op1=ALU.add)
        P.vop("tensor_tensor_scan", out=sb_S2[:, :], data0=sb_A[:, :], data1=sb_S2[:, :],
              initial=0.0, op0=ALU.mult, op1=ALU.add)
        P.vop("tensor_tensor", out=sb_Ev[:, :], in0=sb_F[:, :], in1=sb_v[:, :], op=ALU.mult)
        P.vop("tensor_tensor", out=sb_S1[:, :], in0=sb_S1[:, :], in1=sb_Ev[:, :], op=ALU.add)
        P.vop("tensor_tensor", out=sb_S2[:, :], in0=sb_S2[:, :], in1=sb_F[:, :], op=ALU.add)
        P.vop("reciprocal", out=sb_S2[:, :], in_=sb_S2[:, :])
        P.vop("tensor_tensor", out=sb_S1[:, :], in0=sb_S1[:, :], in1=sb_S2[:, :], op=ALU.mult)
        m_rr = P.vop("tensor_tensor", out=rr3[:, dt, :], in0=sb_S1[:, :], in1=sb_r[:, :],
                     op=ALU.mult)
        wk_prev = [("v", m_rr)]

    sb_rhs1 = P.sbuf("l1_rhs", [128, 16 * TB], BF16)
    rh3 = r3(sb_rhs1[:, :], TB)
    sb_lw1 = P.sbuf("l1_lw", [128, 2 * 16 * 128], BF16)
    lw4 = sb_lw1[:, :].rearrange("p (b h j) -> p b h j", b=2, h=16)
    sb_pst = P.sbuf("l1_pst", [128, TB], F32)
    wsrc = {0: io["wk1t"], 1: io["wv1t"], 2: io["wr1t"]}
    w_pst = None
    lw_prev = [None, None]
    for pi in range(3):
        w_r = P.dma("y", out=r3(sb_rhs1[:, :], TB), in_=tile_rows(d_kxr[pi * D:(pi + 1) * D, :]),
                    waits=P.fence([kx_prev[0], kx_prev[1]]) if pi == 0 else [("t", P.cnt["t"])],
                    sem="drh1")
        for dt in range(8):
            buf = dt % 2
            wl = [lw_prev[buf]] if lw_prev[buf] else (
                P.fence([kx_prev[0], kx_prev[1]]) if pi == 0 else [])
            w_lw = P.dma("g", out=lw4[:, buf, :, :],
                         in_=tile_rows(wsrc[pi][:, dt * 128:(dt + 1) * 128]),
                         waits=wl, sem=f"dlw1{buf}")
            for tc in range(4):
                P.pe_wait([w_lw, w_r] if tc == 0 else [])
                steps = [(lw4[:, buf, ht, :], rh3[:, ht, tc * 512:(tc + 1) * 512])
                         for ht in range(16)]
                p, bi, mk = chain(P, banks, steps)
                if w_pst and tc == 0:
                    P.wait_on("v" if pi != 2 else "s", [w_pst])
                if pi == 2:
                    hcm = P.sop("activation", waits=[("t", mk)],
                                out=sb_pst[:, tc * 512:(tc + 1) * 512], in_=p[:, 0:512],
                                func=AF.Sigmoid)
                    banks.release(bi, ("s", hcm))
                else:
                    hcm = P.vop("tensor_copy", waits=[("t", mk)],
                                out=sb_pst[:, tc * 512:(tc + 1) * 512], in_=p[:, 0:512])
                    banks.release(bi, ("v", hcm))
            lw_prev[buf] = ("t", P.cnt["t"])
            w_pst = P.dma("y", out=d_kvr[pi * 1024 + dt * 128:pi * 1024 + (dt + 1) * 128, :],
                          in_=sb_pst[:, :],
                          waits=[("s" if pi == 2 else "v", hcm)], sem="dpst")
            if pi == 2:
                emit_scan(dt, w_pst)
    sb_rhs1.free(); sb_lw1.free(); sb_pst.free()
    w_pst_last = w_pst
    for t in (sb_k, sb_v, sb_r, sb_E, sb_F, sb_Ev, sb_S1, sb_S2, sb_A, sb_m):
        t.free()

    sb_wo1 = P.sbuf("l1_wo1", [128, 8 * D], BF16)
    w_wo1 = P.dma("g", out=r3(sb_wo1[:, :], D), in_=tile_rows(io["wo1"][:, :]),
                  sem="dwo1", waits=P.fence())
    wo13 = r3(sb_wo1[:, :], D)
    sb_ost = P.sbuf("l1_ost", [128, D], BF16)
    sb_xp = P.sbuf("l1f_xp", [128, 16 * TOWN], F32)
    xp3 = r3(sb_xp[:, :], TOWN)
    sb_bsc = P.sbuf("l1f_bsc", [128, TOWN], F32)
    sb_y = P.sbuf("l1f_y", [128, 16 * TOWN], BF16)
    y3 = r3(sb_y[:, :], TOWN)
    sb_xin = P.sbuf("l1f_xin", [128, 2 * TOWN], BF16)
    xin3 = r3(sb_xin[:, :], TOWN)
    xin_m = [0, 0]
    m_cast = [0] * 16
    m_h128 = 0

    def job_xp(ft):
        buf = ft % 2
        w = [w_pst_last, tps_w[0], tps_w[1]] if ft < 2 else []
        if xin_m[buf]:
            w.append(("v", xin_m[buf]))
        wld = P.dma("y", out=xin3[:, buf, :], in_=d_xn1T_in[ft * 128:(ft + 1) * 128, :],
                    waits=w, sem=f"dxi{buf}")
        xin_m[buf] = P.vop("tensor_scalar", waits=[wld], out=xp3[:, ft, :],
                           in0=xin3[:, buf, :], scalar1=vec3[:, 5, ft:ft + 1],
                           scalar2=None, op0=ALU.mult)

    def job_bfly(a, b):
        P.vop("tensor_tensor", out=sb_bsc[:, :], in0=a, in1=b, op=ALU.subtract)
        P.vop("tensor_tensor", out=a, in0=a, in1=b, op=ALU.add)
        P.vop("tensor_copy", out=b, in_=sb_bsc[:, :])

    def job_cast(ft):
        m_cast[ft] = P.vop("tensor_copy", out=y3[:, ft, :], in_=xp3[:, ft, :])

    def job_h128(ft, tc):
        nonlocal m_h128
        P.pe_wait(([("v", m_cast[ft])] if tc == 0 else []) +
                  ([w_h] if (ft == 0 and tc == 0) else []))
        bi = banks.acquire()
        p = banks.ps[bi]
        P.pe("matmul", out=p[:, 0:512], lhsT=sb_h128[:, :],
             rhs=y3[:, ft, tc * 512:(tc + 1) * 512], start=True, stop=True, inc=True)
        m_h128 = P.vop("tensor_copy", waits=[("t", P.cnt["t"])],
                       out=y3[:, ft, tc * 512:(tc + 1) * 512], in_=p[:, 0:512])
        banks.release(bi, ("v", m_h128))

    vec_jobs = [(lambda ft=ft: job_xp(ft)) for ft in range(16)]
    for lvl in range(4):
        S = 1 << lvl
        for base in range(0, 16, 2 * S):
            for k in range(S):
                vec_jobs.append(lambda a=xp3[:, base + k, :],
                                b=xp3[:, base + k + S, :]: job_bfly(a, b))
    vec_jobs.extend((lambda ft=ft: job_cast(ft)) for ft in range(16))
    h_jobs = [(ft, tc) for ft in range(16) for tc in range(2)]
    vj = hj = 0

    P.pe_wait([w_wo1])
    w_at1 = None
    for tt in range(16):
        if w_at1:
            P.wait_on("v", [w_at1])
        hcm = 0
        for hc in range(4):
            steps = [(rr3[:, dt, tt * 128:(tt + 1) * 128],
                      wo13[:, dt, hc * 512:(hc + 1) * 512]) for dt in range(8)]
            p, bi, mk = chain(P, banks, steps)
            hcm = P.vop("tensor_copy", waits=[("t", mk)],
                        out=sb_ost[:, hc * 512:(hc + 1) * 512], in_=p[:, 0:512])
            banks.release(bi, ("v", hcm))
        w_at1 = P.dma("y", out=d_at1_in[tt * 128:(tt + 1) * 128, :], in_=sb_ost[:, :],
                      waits=[("v", hcm)], sem="dat1")
        for _ in range(6):
            if vj < len(vec_jobs):
                vec_jobs[vj]()
                vj += 1
        if vj >= len(vec_jobs):
            for _ in range(4):
                if hj < len(h_jobs):
                    job_h128(*h_jobs[hj])
                    hj += 1
    w_rs_at1 = P.coll("ReduceScatter", ALU.add, PAIRS, [d_at1_in[:, :].opt()],
                      [d_at1[:, :].opt()], waits=[w_at1])
    while vj < len(vec_jobs):
        vec_jobs[vj]()
        vj += 1
    while hj < len(h_jobs):
        job_h128(*h_jobs[hj])
        hj += 1
    m_h_last = m_h128
    sb_wo1.free(); sb_ost.free(); sb_rrT.free()
    sb_xp.free(); sb_bsc.free(); sb_xin.free()
    ys3 = y3

    sb_wsc = P.sbuf("l1f_wsc", [128, 128], F32)
    sb_sdp = P.sbuf("l1f_sdp", [128, 64], F32)
    w_w1 = P.dma("y", out=sb_wsc[:, :], in_=io["wscug128"][:, :], sem="dcl2",
                 waits=P.fence([w_at1, w_pst_last]))
    P.dma("y", out=sb_sdp[:, :], in_=io["sudp64"][:, :], sem="dcl2")
    w_w2 = None

    sb_of = P.sbuf("l1f_onesf", [1, 128], F32)
    sb_dprow = P.sbuf("l1f_dprow", [1, D], F32)
    sb_osc = P.sbuf("l1f_osc", [128, D], F32)
    m_o1 = P.op("g", "memset", waits=P.fence([w_at1, w_pst_last]), ap=sb_of[:, :],
                constant=1.0)
    P.dma("y", out=sb_dprow[:, :], in_=io["wscdp_row"][:, :], sem="dcl2")
    w_cl2 = ("dcl2", P.cnt["dcl2"])
    P.vop("tensor_scalar", waits=[w_cl2], out=sb_wsc[:, 64:128], in0=sb_wsc[:, 64:128],
          scalar1=64.0, scalar2=None, op0=ALU.mult)
    P.vop("tensor_tensor", out=sb_sdp[:, :], in0=sb_sdp[:, :], in1=sb_wsc[:, 0:64],
          op=ALU.mult)
    m_scl = P.vop("tensor_scalar", out=sb_sdp[:, :], in0=sb_sdp[:, :],
                  scalar1=SC_UP, scalar2=None, op0=ALU.mult)
    P.pe_wait([("g", m_o1), w_cl2])
    for hc in range(4):
        bi = banks.acquire()
        p = banks.ps[bi]
        P.pe("matmul", out=p[:, 0:512], lhsT=sb_of[0:1, :],
             rhs=sb_dprow[0:1, hc * 512:(hc + 1) * 512], start=True, stop=True, inc=True)
        m_osc = P.vop("tensor_scalar", waits=[("t", P.cnt["t"])],
                      out=sb_osc[:, hc * 512:(hc + 1) * 512], in0=p[:, 0:512],
                      scalar1=SC_OUT, scalar2=None, op0=ALU.mult)
        banks.release(bi, ("v", m_osc))

    sb_acc = P.sbuf("l1f_acc", [128, 8 * D], F32)
    ac3 = r3(sb_acc[:, :], D)
    sb_lwu = P.sbuf("l1f_lwu", [128, 4 * 16 * 128], BF16)
    lwu4 = sb_lwu[:, :].rearrange("p (b h j) -> p b h j", b=4, h=16)
    sb_y2 = P.sbuf("l1f_y2", [128, 8 * 512], BF16)
    y23 = r3(sb_y2[:, :], 512)
    sb_wdt = P.sbuf("l1f_wdt", [128, 8 * D], BF16)
    wdt3 = r3(sb_wdt[:, :], D)
    sb_sil = P.sbuf("l1f_sil", [128, 512], F32)
    lwu_prev = [None] * 4
    wdt_prev = None
    m_sil_prev = 0
    pend = None

    def h128_flush():
        nonlocal pend
        if pend is None:
            return
        jp, m_gpp = pend
        P.pe_wait([("v", m_gpp)])
        bi = banks.acquire()
        p = banks.ps[bi]
        P.pe("matmul", out=p[:, 0:512], lhsT=sb_h128[:, :],
             rhs=sb_y2[:, jp * 512:(jp + 1) * 512], start=True, stop=True, inc=True)
        m_y2 = P.vop("tensor_copy", waits=[("t", P.cnt["t"])],
                     out=sb_y2[:, jp * 512:(jp + 1) * 512], in_=p[:, 0:512])
        banks.release(bi, ("v", m_y2))
        pend = None

    for th2 in range(2):
        toff = th2 * 512
        for sup in range(8):
            w_wdt = None
            for j in range(8):
                f2 = sup * 8 + j
                wd_w = [wdt_prev] if (wdt_prev and j == 0) else []
                if sup == 0 and th2 == 0 and j == 0:
                    wd_w = P.fence([io["w_ag_d"], w_at1, w_pst_last])
                w_wdt = P.dma("y", out=wdt3[:, j, :], in_=io["d_w1d"][:, f2 * 128:(f2 + 1) * 128],
                              waits=wd_w, sem="dwdt", tp=True)
            for j in range(8):
                f2 = sup * 8 + j
                nu, ng = f2, 64 + f2
                bu_, bg_ = (2 * j) % 4, (2 * j + 1) % 4
                wlu = [lwu_prev[bu_]] if lwu_prev[bu_] else []
                if sup == 0 and th2 == 0 and j == 0:
                    wlu = P.fence([io["w_ag_ug"], w_at1, w_pst_last])
                elif sup == 0 and th2 == 0 and j == 1:
                    wlu = wlu + [w_at1, w_pst_last]
                w_lu = P.dma("g", out=lwu4[:, bu_, :, :],
                             in_=tile_rows(io["d_w1ug"][(nu // 16) * D:(nu // 16 + 1) * D,
                                                        (nu % 16) * 128:(nu % 16 + 1) * 128]),
                             waits=wlu, sem=f"dlu{bu_}")
                wlg = [lwu_prev[bg_]] if lwu_prev[bg_] else (
                    [w_at1, w_pst_last] if (sup == 0 and th2 == 0 and j < 2) else [])
                w_lg = P.dma("g", out=lwu4[:, bg_, :, :],
                             in_=tile_rows(io["d_w1ug"][(ng // 16) * D:(ng // 16 + 1) * D,
                                                        (ng % 16) * 128:(ng % 16 + 1) * 128]),
                             waits=wlg, sem=f"dlu{bg_}")
                P.pe_wait([w_lu, w_lg] + ([("v", m_scl), ("v", m_h_last)]
                                          if (sup == 0 and th2 == 0 and j == 0) else []))
                stepsu = [(lwu4[:, bu_, ft, :], ys3[:, ft, toff:toff + 512]) for ft in range(16)]
                pu, biu, mku = chain(P, banks, stepsu)
                stepsg = [(lwu4[:, bg_, ft, :], ys3[:, ft, toff:toff + 512]) for ft in range(16)]
                pg, big, mkg = chain(P, banks, stepsg)
                lwu_prev[bu_] = ("t", P.cnt["t"])
                lwu_prev[bg_] = ("t", P.cnt["t"])
                h128_flush()
                m_sil = P.sop("activation", waits=[("t", mkg), ("v", m_sil_prev)],
                              out=sb_sil[:, :], in_=pg[:, 0:512],
                              func=AF.Silu, scale=sb_wsc[:, 64 + f2:65 + f2])
                m_gg = P.vop("tensor_tensor", waits=[("s", m_sil), ("t", mku)],
                             out=sb_sil[:, :],
                             in0=sb_sil[:, :], in1=pu[:, 0:512], op=ALU.mult)
                m_gp = P.vop("tensor_scalar", out=sb_y2[:, j * 512:(j + 1) * 512],
                             in0=sb_sil[:, :], scalar1=sb_sdp[:, f2:f2 + 1], scalar2=None,
                             op0=ALU.mult)
                banks.release(big, ("s", m_sil))
                banks.release(biu, ("v", m_gg))
                m_sil_prev = m_gp
                pend = (j, m_gp)
            h128_flush()
            for tt4 in range(4):
                for hc in range(4):
                    P.pe_wait([w_wdt, ("v", P.cnt["v"])] if (tt4 == 0 and hc == 0) else [])
                    steps = [(y23[:, j, tt4 * 128:(tt4 + 1) * 128],
                              wdt3[:, j, hc * 512:(hc + 1) * 512]) for j in range(8)]
                    p, bi, mk = chain(P, banks, steps)
                    dst = ac3[:, th2 * 4 + tt4, hc * 512:(hc + 1) * 512]
                    if sup == 0:
                        m_ac = P.vop("tensor_copy", waits=[("t", mk)], out=dst, in_=p[:, 0:512])
                    else:
                        m_ac = P.vop("tensor_tensor", waits=[("t", mk)], out=dst,
                                     in0=dst, in1=p[:, 0:512], op=ALU.add)
                    banks.release(bi, ("v", m_ac))
            wdt_prev = ("t", P.cnt["t"])
    m_ffn = P.cnt["v"]
    sb_y.free(); sb_lwu.free(); sb_y2.free(); sb_wdt.free(); sb_sil.free()

    y1 = io["y1"]
    sb_fx = P.sbuf("l1r_x", [128, D], F32)
    sb_fa = P.sbuf("l1r_a", [128, D], BF16)
    w_out = None
    for i in range(8):
        wx = []
        if i == 0:
            wx = P.fence([w_rs_at1])
        x1src_d = io["x1"] if "x1" in io else io["d_x1"]
        wx1 = P.dma("y", out=sb_fx[:, :], in_=x1src_d[i * 128:(i + 1) * 128, :],
                    waits=wx + ([w_out] if w_out else []), sem="dfx")
        wa1 = P.dma("y", out=sb_fa[:, :], in_=d_at1[i * 128:(i + 1) * 128, :],
                    waits=[], sem="dfx")
        P.vop("tensor_tensor", waits=[wx1, wa1, ("v", m_ffn)],
              out=ac3[:, i, :], in0=ac3[:, i, :], in1=sb_osc[:, :], op=ALU.mult)
        P.vop("tensor_tensor", out=ac3[:, i, :], in0=ac3[:, i, :], in1=sb_fx[:, :], op=ALU.add)
        m_f = P.vop("tensor_tensor", out=ac3[:, i, :], in0=ac3[:, i, :], in1=sb_fa[:, :],
                    op=ALU.add)
        w_out = P.dma("y", out=y1[i * 128:(i + 1) * 128, :], in_=ac3[:, i, :],
                      waits=[("v", m_f)], sem="dy1")
    P.wait_on("y", [w_out] + [(f"cc{i}", 1) for i in range(P.n_cc)])


def hadamard(n):
    h = np.array([[1.0]], np.float32)
    while h.shape[0] < n:
        h = np.block([[h, h], [h, -h]])
    return h


def make_consts():
    c = {}
    c["c_ident_in"] = np.eye(128).astype(BF16NP)
    mask = np.where(np.arange(128)[:, None] > np.arange(128)[None, :], MASK_NEG, 0.0)
    c["c_mask_in"] = mask.astype(np.float32)
    c["c_ones_in"] = np.ones((128, 128), BF16NP)
    freqs = 1.0 / (10000.0 ** (np.arange(0, HD, 2, dtype=np.float64) / HD))
    ang = np.arange(TB, dtype=np.float64)[:, None] * freqs[None, :]
    c["rope_cos"] = np.cos(ang).astype(np.float32)
    c["rope_sin"] = np.sin(ang).astype(np.float32)
    c["c_h128_in"] = hadamard(128).astype(BF16NP)
    return c


def make_l0_inputs(full, consts):
    x = np.asarray(full["L0_attn_x"])
    wqkv = np.asarray(full["L0_attn_Wqkv"])
    wo = np.asarray(full["L0_attn_Wo"])
    gpk = np.asarray(full["L0_ffn_gate_W_packed"])
    gsc = np.asarray(full["L0_ffn_gate_scales"])
    upk = np.asarray(full["L0_ffn_up_W_packed"])
    usc = np.asarray(full["L0_ffn_up_scales"])
    dpk = np.asarray(full["L0_ffn_down_W_packed"])
    dsc = np.asarray(full["L0_ffn_down_scales"])
    cores = []
    for c in range(NC):
        b, h = c // 2, c % 2
        d = dict(consts)
        d["xb"] = np.ascontiguousarray(x[b])
        d["xo"] = np.ascontiguousarray(x[b, h * TOWN:(h + 1) * TOWN])
        d["wq"] = np.ascontiguousarray(wqkv[:, h * 1024:(h + 1) * 1024])
        d["wk"] = np.ascontiguousarray(wqkv[:, 2048 + 256 * h:2048 + 256 * h + 256])
        d["wv"] = np.ascontiguousarray(wqkv[:, 2560 + 256 * h:2560 + 256 * h + 256])
        d["wo0"] = np.ascontiguousarray(wo[h * 1024:(h + 1) * 1024, :])
        d["gpk"] = np.ascontiguousarray(gpk[c * 2048:(c + 1) * 2048, :])
        d["gsc"] = np.ascontiguousarray(gsc[c * 2048:(c + 1) * 2048, :])
        d["upk"] = np.ascontiguousarray(upk[c * 2048:(c + 1) * 2048, :])
        d["usc"] = np.ascontiguousarray(usc[c * 2048:(c + 1) * 2048, :])
        d["dpk"] = np.ascontiguousarray(dpk[:, c * 1024:(c + 1) * 1024])
        d["dsc"] = np.ascontiguousarray(dsc[:, c * 64:(c + 1) * 64])
        cores.append(d)
    return cores


L0_IN_SPECS = [
    ("xb", [TB, D], F32), ("xo", [TOWN, D], F32),
    ("wq", [D, 1024], F32), ("wk", [D, 256], F32), ("wv", [D, 256], F32),
    ("wo0", [1024, D], F32),
    ("gpk", [2048, 1024], I32), ("gsc", [2048, 64], F32),
    ("upk", [2048, 1024], I32), ("usc", [2048, 64], F32),
    ("dpk", [2048, 1024], I32), ("dsc", [2048, 64], F32),
    ("c_ident_in", [128, 128], BF16), ("c_mask_in", [128, 128], F32),
    ("c_ones_in", [128, 128], BF16),
    ("rope_cos", [TB, 64], F32), ("rope_sin", [TB, 64], F32),
    ("c_h128_in", [128, 128], BF16),
]


def build_l0(stage=99):
    P = Prog()
    io = {"stage": stage}
    for name, shape, dt in L0_IN_SPECS:
        io[name] = P.dram(name, shape, dt, kind="ExternalInput")
    y0 = P.dram("y0", [TOWN, D], F32, kind="ExternalOutput")
    io["y0"] = y0
    emit_l0(P, io)
    if stage >= 99:
        finish_l0_only(P, io, y0)
    nc = P.build()
    return nc, P


def l0_forward(full_inputs, stage=99):
    consts = make_consts()
    cores = make_l0_inputs(full_inputs, consts)
    nc, P = build_l0(stage)
    res = run_bass_kernel_spmd(nc, cores, core_ids=list(range(NC)))
    out = np.zeros((4, TB, D), np.float32)
    for c in range(NC):
        b, h = c // 2, c % 2
        out[b, h * TOWN:(h + 1) * TOWN] = np.asarray(res.results[c]["y0"])
    return out




L1_IN_SPECS = [
    ("x1", [TOWN, D], F32),
    ("wk1t", [D, 1024], F32), ("wv1t", [D, 1024], F32), ("wr1t", [D, 1024], F32),
    ("wo1", [1024, D], F32),
    ("td8", [128, 8], F32), ("tf8", [128, 8], F32),
    ("tmk16", [128, 16], F32), ("tmv16", [128, 16], F32), ("tmr16", [128, 16], F32),
    ("w1ugt", [D, D], F32),
    ("w1d", [256, N1], F32),
    ("suug16", [128, 16], F32), ("sudp64", [128, 64], F32),
    ("wscug128", [128, 128], F32), ("wscdp_row", [1, D], F32),
    ("c_ident_in", [128, 128], BF16), ("c_h128_in", [128, 128], BF16),
]


def make_l1_inputs(full, consts):
    Wk = np.asarray(full["L1_attn_Wk"])
    Wv = np.asarray(full["L1_attn_Wv"])
    Wr = np.asarray(full["L1_attn_Wr"])
    Wo = np.asarray(full["L1_attn_Wo"])
    td = np.asarray(full["L1_attn_time_decay"]).reshape(-1)
    tf = np.asarray(full["L1_attn_time_first"]).reshape(-1)
    tmk = np.asarray(full["L1_attn_time_mix_key"]).reshape(-1)
    tmv = np.asarray(full["L1_attn_time_mix_value"]).reshape(-1)
    tmr = np.asarray(full["L1_attn_time_mix_receptance"]).reshape(-1)
    W_up = np.asarray(full["L1_ffn_W_up"])
    W_gate = np.asarray(full["L1_ffn_W_gate"])
    W_down = np.asarray(full["L1_ffn_W_down"])
    su_ug = np.asarray(full["L1_ffn_SU_ug"]).reshape(-1)
    su_dp = np.asarray(full["L1_ffn_SU_dp"]).reshape(-1)
    wsc_ug = np.asarray(full["L1_ffn_Wscale_ug"]).reshape(-1)
    wsc_dp = np.asarray(full["L1_ffn_Wscale_dp"]).reshape(-1)
    W_ug = np.concatenate([W_up, W_gate], 0)
    cores = []
    for c in range(NC):
        b, h = c // 2, c % 2
        dmy = slice(1024 * h, 1024 * (h + 1))
        d = {}
        d["wk1t"] = np.ascontiguousarray(Wk[dmy].T)
        d["wv1t"] = np.ascontiguousarray(Wv[dmy].T)
        d["wr1t"] = np.ascontiguousarray(Wr[dmy].T)
        d["wo1"] = np.ascontiguousarray(Wo[:, dmy].T)
        d["td8"] = np.ascontiguousarray(td[dmy].reshape(8, 128).T)
        d["tf8"] = np.ascontiguousarray(tf[dmy].reshape(8, 128).T)
        d["tmk16"] = np.ascontiguousarray(tmk.reshape(16, 128).T)
        d["tmv16"] = np.ascontiguousarray(tmv.reshape(16, 128).T)
        d["tmr16"] = np.ascontiguousarray(tmr.reshape(16, 128).T)
        d["w1ugt"] = np.ascontiguousarray(W_ug[c * 2048:(c + 1) * 2048, :].T)
        d["w1d"] = np.ascontiguousarray(W_down[c * 256:(c + 1) * 256, :])
        d["suug16"] = np.ascontiguousarray(su_ug.reshape(16, 128).T)
        d["sudp64"] = np.ascontiguousarray(su_dp.reshape(64, 128).T)
        d["wscug128"] = np.ascontiguousarray(wsc_ug.reshape(128, 128).T)
        d["wscdp_row"] = np.ascontiguousarray(wsc_dp.reshape(1, D))
        d["c_ident_in"] = consts["c_ident_in"]
        d["c_h128_in"] = consts["c_h128_in"]
        cores.append(d)
    return cores


def build_l1(stage=99):
    P = Prog()
    io = {"stage": stage}
    for name, shape, dt in L1_IN_SPECS:
        io[name] = P.dram(name, shape, dt, kind="ExternalInput")
    y1 = P.dram("y1", [TOWN, D], F32, kind="ExternalOutput")
    io["y1"] = y1
    ps = [P.psum(f"ps{i}", [128, 512], F32) for i in range(7)]
    ps_tp = P.psum("ps_tp", [128, 1024], BF16)
    sb_ident = P.sbuf("c_ident", [128, 128], BF16)
    P.dma("y", out=sb_ident[:, :], in_=io["c_ident_in"][:, :], sem="dcst")
    io.update(ps=ps, ps_tp=ps_tp, banks=Banks(P, ps[0:5]), sb_ident=sb_ident)
    P.wait_on("t", [("dcst", 16)])
    emit_l1_prep(P, io, part="both")
    emit_l1(P, io)
    nc = P.build()
    return nc, P


def l1_forward(x1_full, l1_cores, stage=99):
    nc, P = build_l1(stage)
    for c in range(NC):
        b, h = c // 2, c % 2
        l1_cores[c]["x1"] = np.ascontiguousarray(x1_full[b, h * TOWN:(h + 1) * TOWN])
    res = run_bass_kernel_spmd(nc, l1_cores, core_ids=list(range(NC)))
    out = np.zeros((4, TB, D), np.float32)
    for c in range(NC):
        b, h = c // 2, c % 2
        out[b, h * TOWN:(h + 1) * TOWN] = np.asarray(res.results[c]["y1"])
    return out


FUSED_IN_SPECS = L0_IN_SPECS + [s for s in L1_IN_SPECS
                                if s[0] not in ("x1", "c_ident_in", "c_h128_in")]


def build_fused():
    P = Prog()
    io = {}
    for name, shape, dt in FUSED_IN_SPECS:
        io[name] = P.dram(name, shape, dt, kind="ExternalInput")
    y1 = P.dram("y1", [TOWN, D], F32, kind="ExternalOutput")
    io["y1"] = y1
    io["early_prep"] = True
    emit_l0(P, io)
    for t in io["res_free"]:
        t.free()
    emit_l1(P, io)
    nc = P.build()
    return nc, P


def kernel(**inputs):
    consts = make_consts()
    c0 = make_l0_inputs(inputs, consts)
    c1 = make_l1_inputs(inputs, consts)
    cores = []
    for c in range(NC):
        d = dict(c0[c])
        d.update({k: v for k, v in c1[c].items() if k != "x1"})
        cores.append(d)
    nc, P = build_fused()
    res = run_bass_kernel_spmd(nc, cores, core_ids=list(range(NC)))
    out = np.zeros((4, TB, D), np.float32)
    for c in range(NC):
        b, h = c // 2, c % 2
        out[b, h * TOWN:(h + 1) * TOWN] = np.asarray(res.results[c]["y1"])
    return out


def kernel_2launch(**inputs):
    consts = make_consts()
    c0 = make_l0_inputs(inputs, consts)
    nc0, _ = build_l0()
    res0 = run_bass_kernel_spmd(nc0, c0, core_ids=list(range(NC)))
    c1 = make_l1_inputs(inputs, consts)
    for c in range(NC):
        c1[c]["x1"] = np.asarray(res0.results[c]["y0"])
    nc1, _ = build_l1()
    res1 = run_bass_kernel_spmd(nc1, c1, core_ids=list(range(NC)))
    out = np.zeros((4, TB, D), np.float32)
    for c in range(NC):
        b, h = c // 2, c % 2
        out[b, h * TOWN:(h + 1) * TOWN] = np.asarray(res1.results[c]["y1"])
    return out


# version 11 (final)
# speedup vs baseline: 1.0725x; 1.0725x over previous
import contextlib
import math
import numpy as np
import ml_dtypes

import concourse.bass as bass
import concourse.mybir as mybir
from concourse.bass_utils import run_bass_kernel_spmd

F32 = mybir.dt.float32
BF16 = mybir.dt.bfloat16
I32 = mybir.dt.int32
AF = mybir.ActivationFunctionType
ALU = mybir.AluOpType
AX = mybir.AxisListType
BF16NP = ml_dtypes.bfloat16

NC = 8
D = 2048
TOWN = 1024
TB = 2048
HD = 128
NHMY = 8
RS_SUB = 256
RS_CALLS = 4

PAIRS = [[0, 1], [2, 3], [4, 5], [6, 7]]
ALL8 = [list(range(NC))]
MASK_NEG = -10000.0
ISQH = 1.0 / math.sqrt(HD)


class _SB:

    def __init__(self, ap2d, prog, off, nbytes):
        self._ap = ap2d
        self._prog = prog
        self._off = off
        self._nbytes = nbytes

    def __getitem__(self, k):
        return self._ap[k]

    def free(self):
        if self._nbytes:
            self._prog._arena_free(self._off, self._nbytes)
            self._prog.sbuf_live -= self._nbytes
            self._nbytes = 0


class Prog:
    ARENA_BYTES = 198 * 1024

    def __init__(self):
        self.nc = bass.Bass("TRN2", target_bir_lowering=False)
        self.body = {e: [] for e in "vstgy"}
        self._cms = []
        self.sems = {}
        self.cnt = {}
        self.sbuf_live = 0
        self.sbuf_peak = 0
        self.n_cc = 0
        self.tp_guard = 0

    def sem(self, name):
        if name not in self.sems:
            cm = self.nc.semaphore(name)
            self.sems[name] = cm.__enter__()
            self._cms.append(cm)
            self.cnt[name] = 0
        return self.sems[name]

    def _arena_t(self):
        if not hasattr(self, "_arena"):
            cm = self.nc.sbuf_tensor("arena", [128, self.ARENA_BYTES], mybir.dt.uint8)
            self._arena = cm.__enter__()
            self._cms.append(cm)
            self._free = [(0, self.ARENA_BYTES)]
        return self._arena

    def _arena_alloc(self, nbytes):
        nbytes = (nbytes + 31) // 32 * 32
        for idx, (off, ln) in enumerate(self._free):
            if ln >= nbytes:
                if ln == nbytes:
                    self._free.pop(idx)
                else:
                    self._free[idx] = (off + nbytes, ln - nbytes)
                return off, nbytes
        raise RuntimeError(f"arena OOM: need {nbytes}, free={self._free}")

    def _arena_free(self, off, nbytes):
        self._free.append((off, nbytes))
        self._free.sort()
        merged = []
        for o, l in self._free:
            if merged and merged[-1][0] + merged[-1][1] == o:
                merged[-1] = (merged[-1][0], merged[-1][1] + l)
            else:
                merged.append([o, l])
        self._free = [tuple(x) for x in merged]

    def sbuf(self, name, shape, dt):
        assert len(shape) == 2 and shape[0] <= 128
        t = self._arena_t()
        esz = mybir.dt.size(dt)
        off, nbytes = self._arena_alloc(int(shape[1]) * esz)
        ap = t[:, off:off + int(shape[1]) * esz].bitcast(dt)
        if shape[0] != 128:
            ap = ap[0:shape[0], :]
        self.sbuf_live += nbytes
        self.sbuf_peak = max(self.sbuf_peak, self.sbuf_live)
        return _SB(ap, self, off, nbytes)

    def psum(self, name, shape, dt):
        cm = self.nc.psum_tensor(name, list(shape), dt)
        t = cm.__enter__()
        self._cms.append(cm)
        return t

    def dram(self, name, shape, dt, kind=None, shared=False):
        if kind:
            return self.nc.dram_tensor(name, list(shape), dt, kind=kind).ap()
        if shared:
            return self.nc.dram_tensor(name, list(shape), dt, addr_space="Shared").ap()
        return self.nc.dram_tensor(name, list(shape), dt).ap()

    def emit(self, e, fn):
        self.body[e].append(fn)

    def _waits(self, eng, waits):
        merged = {}
        for s, v in waits:
            merged[s] = max(merged.get(s, 0), v)
        for s, v in merged.items():
            if v > 0:
                eng.wait_ge(self.sems[s], v)

    def op(self, e, name, waits=(), **kw):
        assert e in "vsg"
        self.sem(e)
        prev = self.cnt[e]
        self.cnt[e] += 1
        waits = tuple(waits)

        def fn(eng, prev=prev, waits=waits, name=name, kw=kw, e=e):
            self._waits(eng, waits)
            if prev > 0:
                eng.wait_ge(self.sems[e], prev)
            getattr(eng, name)(**kw).then_inc(self.sems[e], 1)
        self.emit(e, fn)
        return self.cnt[e]

    def vop(self, name, waits=(), **kw):
        return self.op("v", name, waits, **kw)

    def sop(self, name, waits=(), **kw):
        return self.op("s", name, waits, **kw)

    def pe(self, name, waits=(), inc=False, **kw):
        self.sem("t")
        if inc:
            self.cnt["t"] += 1

        def fn(eng, waits=tuple(waits), name=name, kw=kw, inc=inc):
            self._waits(eng, waits)
            inst = getattr(eng, name)(**kw)
            if inc:
                inst.then_inc(self.sems["t"], 1)
        self.emit("t", fn)
        return self.cnt["t"]

    def pe_wait(self, waits):
        if not waits:
            return

        def fn(eng, waits=tuple(waits)):
            self._waits(eng, waits)
        self.emit("t", fn)

    def wait_on(self, e, waits):
        def fn(eng, waits=tuple(waits)):
            self._waits(eng, waits)
        self.emit(e, fn)

    def dma(self, q, out, in_, waits=(), sem=None, tp=False):
        sname = sem or ("d" + q)
        self.sem(sname)
        self.cnt[sname] += 16
        mark = self.cnt[sname]

        def fn(eng, waits=tuple(waits), out=out, in_=in_, sname=sname, tp=tp):
            self._waits(eng, waits)
            if tp:
                eng.dma_start(out=out, in_=in_, transpose=True).then_inc(self.sems[sname], 16)
            else:
                eng.dma_start(out=out, in_=in_).then_inc(self.sems[sname], 16)
        self.emit(q, fn)
        return (sname, mark)

    def fence(self, extra=()):
        return [("v", self.cnt.get("v", 0)), ("s", self.cnt.get("s", 0)),
                ("t", self.cnt.get("t", 0))] + list(extra)

    def coll(self, kind, op, groups, ins, outs, waits=()):
        name = f"cc{self.n_cc}"
        self.n_cc += 1
        self.sem(name)

        def fn(eng, waits=tuple(waits), ins=ins, outs=outs, kind=kind, op=op,
               groups=groups, name=name):
            self._waits(eng, waits)
            eng.collective_compute(kind, op, replica_groups=groups,
                                   ins=ins, outs=outs).then_inc(self.sems[name])
        self.emit("g", fn)
        return (name, 1)

    def build(self):
        with contextlib.ExitStack() as stk:
            block = stk.enter_context(self.nc.Block())
            decs = {"v": block.vector, "s": block.scalar, "t": block.tensor,
                    "g": block.gpsimd, "y": block.sync}
            for e in "vstgy":
                ops = list(self.body[e])

                def runner(eng, ops=ops):
                    for fn in ops:
                        fn(eng)
                decs[e](runner)
        for cm in reversed(self._cms):
            cm.__exit__(None, None, None)
        return self.nc


class Banks:
    def __init__(self, P, psums):
        self.P = P
        self.ps = psums
        self.busy = [None] * len(psums)
        self.rr = 0

    def acquire(self):
        i = self.rr % len(self.ps)
        self.rr += 1
        if self.busy[i] is not None:
            self.P.pe_wait([self.busy[i]])
            self.busy[i] = None
        return i

    def release(self, i, eng_mark):
        self.busy[i] = eng_mark


def chain(P, banks, steps, n=512):
    i = banks.acquire()
    ps = banks.ps[i]
    last = len(steps) - 1
    for k, (l, r) in enumerate(steps):
        P.pe("matmul", out=ps[:, 0:n], lhsT=l, rhs=r,
             start=(k == 0), stop=(k == last), inc=(k == last))
    return ps, i, P.cnt["t"]


def r3(ap2d, inner):
    return ap2d.rearrange("p (a b) -> p a b", b=inner)


def tile_rows(dram_ap, p=128):
    return dram_ap.rearrange("(a p) c -> p a c", p=p)


def bcast_free(ap2d_slice, n):
    apdims = [list(d) for d in ap2d_slice.ap]
    return bass.AP(tensor=ap2d_slice.tensor, offset=ap2d_slice.offset,
                   ap=apdims + [[0, n]])


def emit_layernorm(P, src_ap, dst_bf, ntiles, tmp, marks_in=()):
    out_marks = []
    sq, st, eps = tmp["sq"], tmp["st"], tmp["eps"]
    for i in range(ntiles):
        x = src_ap(i)
        w = tuple(marks_in) if i == 0 else ()
        m_sq = P.sop("activation", waits=w, out=sq[:, 0:D], in_=x, func=AF.Square)
        P.vop("tensor_reduce", waits=w, out=st[:, 0:1], in_=x, axis=AX.X, op=ALU.add)
        P.vop("tensor_reduce", waits=[("s", m_sq)], out=st[:, 1:2], in_=sq[:, 0:D], axis=AX.X, op=ALU.add)
        P.vop("tensor_scalar", out=st[:, 0:1], in0=st[:, 0:1], scalar1=1.0 / D, scalar2=None, op0=ALU.mult)
        P.vop("tensor_scalar", out=st[:, 1:2], in0=st[:, 1:2], scalar1=1.0 / D, scalar2=None, op0=ALU.mult)
        P.vop("tensor_tensor", out=st[:, 2:3], in0=st[:, 0:1], in1=st[:, 0:1], op=ALU.mult)
        m_v = P.vop("tensor_tensor", out=st[:, 1:2], in0=st[:, 1:2], in1=st[:, 2:3], op=ALU.subtract)
        P.sop("activation", waits=[("v", m_v)], out=st[:, 2:3], in_=st[:, 1:2], func=AF.Ln, bias=eps[:, 0:1])
        m_r = P.sop("activation", out=st[:, 3:4], in_=st[:, 2:3], func=AF.Exp, scale=-0.5)
        m = P.vop("tensor_scalar", waits=[("s", m_r)], out=dst_bf(i), in0=x,
                  scalar1=st[:, 0:1], scalar2=st[:, 3:4], op0=ALU.subtract, op1=ALU.mult)
        out_marks.append(m)
    return out_marks


def emit_transpose_to(P, src_tile_fn, dst_fn, ngroups, ident, ps_tp, waits_first=(),
                      gsz=8):
    for g in range(ngroups):
        w = list(waits_first) if g == 0 else []
        if P.tp_guard:
            w.append(("v", P.tp_guard))
        P.pe_wait(w)
        for j in range(gsz):
            P.pe("transpose", out=ps_tp[:, j * 128:(j + 1) * 128],
                 in_=src_tile_fn(g, j), identity=ident, inc=(j == gsz - 1))
        mk = P.cnt["t"]
        P.tp_guard = P.vop("tensor_copy", waits=[("t", mk)],
                           out=dst_fn(g), in_=r3(ps_tp[:, 0:gsz * 128], 128))
    return P.tp_guard


def _dbg_finish(P, io):
    y0 = io["y0"]
    t = P.sbuf("dbg_z", [128, D], F32)
    allw = [(s, c) for s, c in P.cnt.items() if c > 0]
    P.wait_on("g", allw)
    m = P.op("g", "memset", ap=t[:, :], constant=0.0)
    P.wait_on("g", [("v", P.cnt["v"]), ("s", P.cnt["s"]), ("t", P.cnt["t"])])
    mm = None
    for i in range(8):
        mm = P.dma("g", out=y0[i * 128:(i + 1) * 128, :], in_=t[:, :],
                   waits=[("g", m)] if i == 0 else (), sem="dout")
    P.wait_on("g", [mm] + [(f"cc{i}", 1) for i in range(P.n_cc)])



def emit_l0(P, io):
    d_xnT_in = P.dram("d_xnT_in", [D, TOWN], BF16)
    d_xnT = P.dram("d_xnT", [NC * D, TOWN], BF16)
    d_wdq = P.dram("d_wdq", [48 * 128, D], BF16)
    d_at0_in = P.dram("d_at0_in", [TB, D], BF16)
    d_at0 = P.dram("d_at0", [TOWN, D], BF16)
    d_part = P.dram("d_part", [RS_CALLS * NC * RS_SUB, D], F32)
    d_ffn = P.dram("d_ffn", [TOWN, D], F32)
    io.update(d_ffn=d_ffn, d_at0=d_at0, d_xnT=d_xnT)

    xb, xo = io["xb"], io["xo"]

    sb_ident = P.sbuf("c_ident", [128, 128], BF16)
    sb_mask = P.sbuf("c_mask", [128, 128], F32)
    sb_ones = P.sbuf("c_ones", [128, 128], BF16)
    sb_cos = P.sbuf("c_cos", [128, 16 * 64], F32)
    sb_sin = P.sbuf("c_sin", [128, 16 * 64], F32)
    sb_eps = P.sbuf("c_eps", [128, 1], F32)
    sb_m16 = P.sbuf("c_m16", [128, 1], F32)
    sb_b4 = P.sbuf("c_b4", [128, 1], F32)
    sb_b6 = P.sbuf("c_b6", [128, 1], F32)
    sb_lnsq = P.sbuf("ln_sq", [128, D], F32)
    sb_lnst = P.sbuf("ln_st", [128, 8], F32)
    ln_tmp = dict(sq=sb_lnsq, st=sb_lnst, eps=sb_eps)
    io["sb_ident"] = sb_ident
    io["sb_ones"] = sb_ones
    io["sb_m16"] = sb_m16

    P.op("g", "memset", ap=sb_eps[:, :], constant=1e-5)
    P.op("g", "memset", ap=sb_m16[:, :], constant=-16.0)
    P.op("g", "memset", ap=sb_b4[:, :], constant=-4.0)
    m_const = P.op("g", "memset", ap=sb_b6[:, :], constant=-6.0)
    io["m_const"] = m_const

    P.dma("y", out=sb_ident[:, :], in_=io["c_ident_in"][:, :], sem="dcst")
    P.dma("y", out=sb_mask[:, :], in_=io["c_mask_in"][:, :], sem="dcst")
    P.dma("y", out=sb_ones[:, :], in_=io["c_ones_in"][:, :], sem="dcst")
    P.dma("y", out=r3(sb_cos[:, :], 64), in_=tile_rows(io["rope_cos"][:, :]), sem="dcst")
    P.dma("y", out=r3(sb_sin[:, :], 64), in_=tile_rows(io["rope_sin"][:, :]), sem="dcst")
    w_cst = ("dcst", P.cnt["dcst"])
    io["w_cst"] = w_cst

    ps = [P.psum(f"ps{i}", [128, 512], F32) for i in range(7)]
    ps_tp = P.psum("ps_tp", [128, 1024], BF16)
    banks = Banks(P, ps[0:5])
    io.update(ps=ps, ps_tp=ps_tp, banks=banks)

    sb_xt = P.sbuf("lnx", [128, 2 * D], F32)
    sb_xnt = P.sbuf("lnxn", [128, D], BF16)
    xt3 = r3(sb_xt[:, :], D)
    ln_state = {"prev": [0, 0]}

    def stream_ln_t(src_dram, ntiles, dst3_fn, first_waits=(), sems=("dxa", "dxb")):
        for i in range(ntiles):
            buf = i % 2
            w = list(first_waits) if i == 0 else []
            if ln_state["prev"][buf]:
                w.append(("v", ln_state["prev"][buf]))
            m_ld = P.dma("y", out=xt3[:, buf, :], in_=src_dram[i * 128:(i + 1) * 128, :],
                         waits=w, sem=sems[buf])
            if i > 0:
                P.wait_on("v", [("t", P.cnt["t"])])
            lnm = emit_layernorm(P, lambda _: xt3[:, buf, :], lambda _: sb_xnt[:, 0:D], 1,
                                 ln_tmp, marks_in=[m_ld])[0]
            ln_state["prev"][buf] = lnm
            emit_transpose_to(
                P, lambda g, j: sb_xnt[:, (g * 8 + j) * 128:(g * 8 + j + 1) * 128],
                lambda g, i=i: dst3_fn(i, g),
                2, sb_ident[:, :], ps_tp, waits_first=[("v", lnm)])
        P.wait_on("v", [("t", P.cnt["t"])])

    sb_xnT = P.sbuf("sb_xnT", [128, 16 * TB], BF16)
    xnT3 = r3(sb_xnT[:, :], TB)
    stream_ln_t(xb[:, :], 16,
                lambda i, g: xnT3[:, g * 8:(g + 1) * 8, i * 128:(i + 1) * 128],
                first_waits=[("g", m_const), w_cst])
    m_xnT = P.cnt["v"]

    sb_wq = P.sbuf("sb_wq", [128, 16 * 1024], BF16)
    sb_wkv = P.sbuf("sb_wkv", [128, 16 * 512], BF16)
    P.dma("g", out=r3(sb_wq[:, :], 1024), in_=tile_rows(io["wq"][:, :]), sem="dwq")
    P.dma("g", out=r3(sb_wkv[:, :], 512)[:, :, 0:256], in_=tile_rows(io["wk"][:, :]), sem="dwk")
    P.dma("g", out=r3(sb_wkv[:, :], 512)[:, :, 256:512], in_=tile_rows(io["wv"][:, :]), sem="dwk")

    sb_V = P.sbuf("sb_V", [128, 16 * 256], BF16)
    V3 = r3(sb_V[:, :], 256)
    wq3 = r3(sb_wq[:, :], 1024)
    wkv3 = r3(sb_wkv[:, :], 512)

    P.pe_wait([("v", m_xnT), ("dwq", 16), ("dwk", 32), w_cst])

    sb_Q = P.sbuf("sb_Q", [128, 16 * 1024], BF16)
    sb_K = P.sbuf("sb_K", [128, 16 * 256], BF16)
    Q3 = r3(sb_Q[:, :], 1024)
    K3 = r3(sb_K[:, :], 256)
    for tt in range(16):
        for dq in range(2):
            steps = [(xnT3[:, ht, tt * 128:(tt + 1) * 128],
                      wq3[:, ht, dq * 512:(dq + 1) * 512]) for ht in range(16)]
            p, i, mk = chain(P, banks, steps)
            m = P.vop("tensor_copy", waits=[("t", mk)],
                      out=Q3[:, tt, dq * 512:(dq + 1) * 512], in_=p[:, 0:512])
            banks.release(i, ("v", m))
        stepskv = [(xnT3[:, ht, tt * 128:(tt + 1) * 128], wkv3[:, ht, 0:512]) for ht in range(16)]
        p, i, mk = chain(P, banks, stepskv)
        m = P.vop("tensor_copy", waits=[("t", mk)], out=K3[:, tt, 0:256], in_=p[:, 0:256])
        m = P.vop("tensor_copy", out=V3[:, tt, :], in_=p[:, 256:512])
        banks.release(i, ("v", m))

    sb_xnT.free(); sb_wq.free(); sb_wkv.free()
    sb_QT = P.sbuf("sb_QT", [128, NHMY * TB], BF16)
    sb_KT = P.sbuf("sb_KT", [128, 2 * TB], BF16)
    QT3 = r3(sb_QT[:, :], TB)
    KT3 = r3(sb_KT[:, :], TB)

    sb_rp = P.sbuf("sb_rp", [128, 1024], F32)
    cosT3 = r3(sb_cos[:, :], 64)
    sinT3 = r3(sb_sin[:, :], 64)

    def rope_tm(t3, tt, nh):
        hd3 = t3[:, tt, :].rearrange("p (h d) -> p h d", d=128)
        lo = hd3[:, :, 0:64]
        hi = hd3[:, :, 64:128]
        rp4 = sb_rp[:, 0:nh * 128].rearrange("p (h d) -> p h d", d=128)
        ra = rp4[:, :, 0:64]
        rb = rp4[:, :, 64:128]
        cosb = bcast_free(cosT3[:, tt, :], nh).transpose([0, 2, 1])
        sinb = bcast_free(sinT3[:, tt, :], nh).transpose([0, 2, 1])
        P.vop("tensor_tensor", out=ra, in0=lo, in1=cosb, op=ALU.mult)
        P.vop("tensor_tensor", out=rb, in0=hi, in1=sinb, op=ALU.mult)
        P.vop("tensor_tensor", out=hi, in0=hi, in1=cosb, op=ALU.mult)
        P.vop("tensor_tensor", out=lo, in0=lo, in1=sinb, op=ALU.mult)
        P.vop("tensor_tensor", out=hi, in0=lo, in1=hi, op=ALU.add)
        P.vop("tensor_tensor", out=lo, in0=ra, in1=rb, op=ALU.subtract)

    for tt in range(16):
        rope_tm(Q3, tt, 8)
        rope_tm(K3, tt, 2)
    m_rope = P.cnt["v"]
    sb_rp.free()

    for tt in range(16):
        emit_transpose_to(
            P, lambda g, j, tt=tt: Q3[:, tt, (g * 8 + j) * 128:(g * 8 + j + 1) * 128],
            lambda g, tt=tt: QT3[:, g * 8:(g + 1) * 8, tt * 128:(tt + 1) * 128],
            1, sb_ident[:, :], ps_tp,
            waits_first=[("v", m_rope)] if tt == 0 else ())
    for tt in range(16):
        emit_transpose_to(
            P, lambda g, j, tt=tt: K3[:, tt, j * 128:(j + 1) * 128],
            lambda g, tt=tt: KT3[:, 0:2, tt * 128:(tt + 1) * 128],
            1, sb_ident[:, :], ps_tp, gsz=2)
    m_tQK = P.tp_guard
    sb_Q.free(); sb_K.free()

    sb_oT = P.sbuf("sb_oT", [128, 16 * TOWN], BF16)
    oT3 = r3(sb_oT[:, :], TOWN)
    stream_ln_t(xo[:, :], 8,
                lambda i, g: oT3[:, g * 8:(g + 1) * 8, i * 128:(i + 1) * 128],
                sems=("dxc", "dxd"))
    m_oT = P.cnt["v"]
    w_oT = P.dma("g", out=tile_rows(d_xnT_in[:, :]), in_=r3(sb_oT[:, :], TOWN),
                 waits=[("v", m_oT)], sem="dxo")
    w_ag1 = P.coll("AllGather", ALU.bypass, ALL8, [d_xnT_in[:, :].opt()], [d_xnT[:, :].opt()],
                   waits=[w_oT])
    io["w_ag1"] = w_ag1
    io.setdefault("reuse_guard", []).extend([w_oT])
    sb_oT.free(); sb_xt.free(); sb_xnt.free()

    deq = _DequantEmitter(P, io, d_wdq, sb_b4, sb_b6, sb_ident, ps_tp,
                          extra_first=io["reuse_guard"] + [("v", P.cnt["v"])])
    io["dq_done"] = deq.done

    sb_PT = P.sbuf("sb_PT", [128, 2 * 16 * 512], BF16)
    sb_bc = P.sbuf("sb_bc", [128, 2 * 512], F32)
    sb_denb = P.sbuf("sb_denb", [1, 2 * 512], BF16)
    sb_aT = P.sbuf("sb_aT", [128, NHMY * TB], BF16)
    aT3 = r3(sb_aT[:, :], TB)
    PT4 = sb_PT[:, :].rearrange("p (g k t) -> p g k t", g=2, k=16)
    bc3 = r3(sb_bc[:, :], 512)
    dnb3 = r3(sb_denb[:, :], 512)

    P.pe_wait([("v", m_tQK)] + io["reuse_guard"])
    blocks = [(hh, qc) for hh in range(NHMY) for qc in range(4)]
    pt_guard = [0, 0]
    pend = None

    def consume(st):
        hh, qc, kts, par, exp_last = st
        kvh = hh // 4
        P.pe_wait([("s", exp_last)])
        bi_pv = banks.acquire()
        pv = banks.ps[bi_pv]
        for k, kt in enumerate(kts):
            P.pe("matmul", out=pv[:, 0:512],
                 lhsT=V3[:, kt, kvh * 128:(kvh + 1) * 128], rhs=PT4[:, par, kt, :],
                 start=(k == 0), stop=(k == len(kts) - 1), inc=(k == len(kts) - 1))
        mk_pv = P.cnt["t"]
        bi_dn = banks.acquire()
        dn = banks.ps[bi_dn]
        for k, kt in enumerate(kts):
            P.pe("matmul", out=dn[0:1, 0:512],
                 lhsT=sb_ones[:, 0:1], rhs=PT4[:, par, kt, :],
                 start=(k == 0), stop=(k == len(kts) - 1), inc=(k == len(kts) - 1))
        mk_dn = P.cnt["t"]
        pt_guard[par] = mk_dn
        P.vop("reciprocal", waits=[("t", mk_dn)], out=bc3[0:1, par, :], in_=dn[0:1, 0:512])
        m_rc = P.vop("tensor_copy", out=dnb3[0:1, par, :], in_=bc3[0:1, par, :])
        banks.release(bi_dn, ("v", m_rc))
        P.pe_wait([("v", m_rc)])
        bi_bc = banks.acquire()
        bc = banks.ps[bi_bc]
        P.pe("matmul", out=bc[:, 0:512], lhsT=sb_ones[0:1, 0:128],
             rhs=dnb3[0:1, par, :], start=True, stop=True, inc=True)
        mk_bc = P.cnt["t"]
        m_bc = P.vop("tensor_copy", waits=[("t", mk_bc)], out=bc3[:, par, :], in_=bc[:, 0:512])
        banks.release(bi_bc, ("v", m_bc))
        m_at = P.vop("tensor_tensor", waits=[("t", mk_pv)],
                     out=aT3[:, hh, qc * 512:(qc + 1) * 512],
                     in0=pv[:, 0:512], in1=bc3[:, par, :], op=ALU.mult)
        banks.release(bi_pv, ("v", m_at))

    for idx, (hh, qc) in enumerate(blocks):
        par = idx % 2
        kvh = hh // 4
        kts = list(range(0, 4 * qc + 4))
        exp_last = 0
        for ki, kt in enumerate(kts):
            bi = banks.acquire()
            pss = banks.ps[bi]
            P.pe("matmul", out=pss[:, 0:512],
                 lhsT=KT3[:, kvh, kt * 128:(kt + 1) * 128],
                 rhs=QT3[:, hh, qc * 512:(qc + 1) * 512],
                 start=True, stop=True, inc=True)
            mk = P.cnt["t"]
            wv_ = [("t", mk)]
            if ki == 0 and pt_guard[par]:
                wv_.append(("t", pt_guard[par]))
            if qc * 512 <= kt * 128 < (qc + 1) * 512:
                off = kt * 128 - qc * 512
                if off > 0:
                    mm = P.vop("tensor_scalar", waits=wv_, out=pss[:, 0:off],
                               in0=pss[:, 0:off], scalar1=MASK_NEG, scalar2=None,
                               op0=ALU.add)
                    wv_ = [("v", mm)]
                mm = P.vop("tensor_tensor", waits=wv_, out=pss[:, off:off + 128],
                           in0=pss[:, off:off + 128], in1=sb_mask[:, :], op=ALU.add)
                wv_ = [("v", mm)]
            if ki == 0 and pt_guard[par]:
                wv_.append(("t", pt_guard[par]))
            exp_last = P.sop("activation", waits=wv_, out=PT4[:, par, kt, :],
                             in_=pss[:, 0:512],
                             func=AF.Exp, bias=sb_m16[:, 0:1], scale=ISQH)
            banks.release(bi, ("s", exp_last))
        st = (hh, qc, kts, par, exp_last)
        if pend is not None:
            consume(pend)
        pend = st
        deq.step()
        deq.step()
    consume(pend)

    while deq.step():
        pass
    sb_PT.free(); sb_bc.free(); sb_denb.free()
    sb_QT.free(); sb_KT.free()

    sb_wo = P.sbuf("sb_wo", [128, 8 * D], BF16)
    wo3 = r3(sb_wo[:, :], D)
    w_wo = P.dma("g", out=r3(sb_wo[:, :], D), in_=tile_rows(io["wo0"][:, :]), sem="dwo",
                 waits=P.fence())
    P.pe_wait([w_wo])
    sb_opst = P.sbuf("sb_opst", [128, D], BF16)
    w_at0 = None
    for tt in range(16):
        if w_at0:
            P.wait_on("v", [w_at0])
        hcm = 0
        for hc in range(4):
            steps = [(aT3[:, hh, tt * 128:(tt + 1) * 128],
                      wo3[:, hh, hc * 512:(hc + 1) * 512]) for hh in range(NHMY)]
            p, i, mk = chain(P, banks, steps)
            hcm = P.vop("tensor_copy", waits=[("t", mk)],
                        out=sb_opst[:, hc * 512:(hc + 1) * 512], in_=p[:, 0:512])
            banks.release(i, ("v", hcm))
        w_at0 = P.dma("y", out=d_at0_in[tt * 128:(tt + 1) * 128, :], in_=sb_opst[:, :],
                      waits=[("v", hcm)], sem="dat")
    w_rs_at0 = P.coll("ReduceScatter", ALU.add, PAIRS, [d_at0_in[:, :].opt()],
                      [d_at0[:, :].opt()], waits=[w_at0])
    io["w_rs_at0"] = w_rs_at0
    sb_opst.free(); sb_aT.free(); sb_V.free(); sb_wo.free()
    deq.free()
    dq_done = deq.done

    sb_cos.free(); sb_sin.free(); sb_lnsq.free(); sb_lnst.free(); sb_mask.free()

    sb_wd = P.sbuf("ffn_wd", [128, 16 * D], BF16)
    sb_rhs = P.sbuf("ffn_rhs", [128, 4 * 16 * RS_SUB], BF16)
    sb_inter = P.sbuf("ffn_inter", [128, 16 * NC * RS_SUB], BF16)
    sb_lw = P.sbuf("ffn_lw", [128, 4 * D], BF16)
    sb_gel = P.sbuf("ffn_gel", [128, RS_SUB], F32)
    sb_dstg = P.sbuf("ffn_dstg", [128, D], F32)
    rhs4 = sb_rhs[:, :].rearrange("p (r h t) -> p r h t", r=4, h=16)
    lw3 = r3(sb_lw[:, :], D)
    wd4 = sb_wd[:, :].rearrange("p (i s j) -> p i s j", i=16, s=16)
    int3 = sb_inter[:, :].rearrange("p (i rt) -> p i rt", i=16)

    w_wd = P.dma("g", out=r3(sb_wd[:, :], D), in_=tile_rows(d_wdq[32 * 128:48 * 128, :]),
                 waits=P.fence([dq_done[47]]), sem="dwd")
    lw_busy = [None] * 4
    rs_waits = []
    w_part = None
    m_int_prev = 0
    for cj in range(RS_CALLS):
        for rh in range(2):
            w_rhs = None
            for rr in range(4):
                r = rh * 4 + rr
                src = d_xnT[r * D:(r + 1) * D, cj * RS_SUB:(cj + 1) * RS_SUB]
                wl = []
                if cj == 0 and rh == 0 and rr == 0:
                    wl = P.fence([io["w_ag1"], dq_done[47]])
                if (rh + cj) and rr == 0:
                    wl.append(("t", P.cnt["t"]))
                w_rhs = P.dma("g", out=rhs4[:, rr, :, :], in_=tile_rows(src),
                              waits=wl, sem="drh")
            for i in range(16):
                bg, bu = (2 * i) % 4, (2 * i + 1) % 4
                wg = P.fence([dq_done[31]]) if (cj == 0 and rh == 0 and i == 0) else []
                if lw_busy[bg] is not None:
                    wg.append(lw_busy[bg])
                w_lg = P.dma("y", out=lw3[:, bg, :], in_=d_wdq[i * 128:(i + 1) * 128, :],
                             waits=wg, sem=f"dlw{bg}")
                wu = [lw_busy[bu]] if lw_busy[bu] is not None else []
                w_lu = P.dma("y", out=lw3[:, bu, :], in_=d_wdq[(16 + i) * 128:(17 + i) * 128, :],
                             waits=wu, sem=f"dlw{bu}")
                for rr in range(4):
                    wpe = [w_lg, w_lu, w_rhs] if rr == 0 else []
                    P.pe_wait(wpe)
                    stepsg = [(lw3[:, bg, ht * 128:(ht + 1) * 128], rhs4[:, rr, ht, :])
                              for ht in range(16)]
                    pg, big, mkg = chain(P, banks, stepsg, n=RS_SUB)
                    stepsu = [(lw3[:, bu, ht * 128:(ht + 1) * 128], rhs4[:, rr, ht, :])
                              for ht in range(16)]
                    pu, biu, mku = chain(P, banks, stepsu, n=RS_SUB)
                    r = rh * 4 + rr
                    m_gel = P.sop("activation", waits=[("t", mkg), ("v", m_int_prev)],
                                  out=sb_gel[:, :],
                                  in_=pg[:, 0:RS_SUB], func=AF.Gelu_apprx_tanh)
                    m_int = P.vop("tensor_tensor", waits=[("s", m_gel), ("t", mku)],
                                  out=int3[:, i, r * RS_SUB:(r + 1) * RS_SUB],
                                  in0=sb_gel[:, :], in1=pu[:, 0:RS_SUB], op=ALU.mult)
                    m_int_prev = m_int
                    banks.release(big, ("s", m_gel))
                    banks.release(biu, ("v", m_int))
                lw_busy[bg] = ("t", P.cnt["t"])
                lw_busy[bu] = ("t", P.cnt["t"])
        m_ph1_v = P.cnt["v"]
        first = True
        for r in range(NC):
            for th in range(2):
                if w_part:
                    P.wait_on("v", [w_part])
                hcm = 0
                for hc in range(4):
                    wpe = []
                    if first:
                        wpe = [("v", m_ph1_v), w_wd]
                        first = False
                    P.pe_wait(wpe)
                    steps = [(int3[:, nt, r * RS_SUB + th * 128:r * RS_SUB + (th + 1) * 128],
                              wd4[:, hc * 4:(hc + 1) * 4, nt, :]) for nt in range(16)]
                    p, bi, mk = chain(P, banks, steps)
                    hcm = P.vop("tensor_copy", waits=[("t", mk)],
                                out=sb_dstg[:, hc * 512:(hc + 1) * 512], in_=p[:, 0:512])
                    banks.release(bi, ("v", hcm))
                row = cj * 2048 + r * RS_SUB + th * 128
                w_part = P.dma("y", out=d_part[row:row + 128, :], in_=sb_dstg[:, :],
                               waits=[("v", hcm)], sem="ddp")
        rs_waits.append(
            P.coll("ReduceScatter", ALU.add, ALL8,
                   [d_part[cj * 2048:(cj + 1) * 2048, :].opt()],
                   [d_ffn[cj * RS_SUB:(cj + 1) * RS_SUB, :].opt()],
                   waits=[w_part]))
    io["w_rs_ffn"] = rs_waits
    for t in (sb_wd, sb_rhs, sb_inter, sb_lw, sb_gel, sb_dstg):
        t.free()
    if io.get("early_prep"):
        io.setdefault("prep_guard", []).append(w_part)
        emit_l1_prep(P, io, part="compute")

    sb_r1 = P.sbuf("res_x", [128, 8 * D], F32)
    sb_r2 = P.sbuf("res_at", [128, 8 * D], BF16)
    sb_r3 = P.sbuf("res_ffn", [128, 8 * D], F32)
    P.dma("y", out=r3(sb_r1[:, :], D), in_=tile_rows(xo[:, :]), sem="dres",
          waits=P.fence([w_part] + io.get("prep_guard", [])))
    P.dma("y", out=r3(sb_r2[:, :], D), in_=tile_rows(d_at0[:, :]),
          waits=[w_rs_at0] + io.get("prep_guard", []), sem="dres")
    P.dma("y", out=r3(sb_r3[:, :], D), in_=tile_rows(d_ffn[:, :]),
          waits=list(rs_waits), sem="dres")
    a1 = r3(sb_r1[:, :], D)
    a2 = r3(sb_r2[:, :], D)
    a3 = r3(sb_r3[:, :], D)
    x1_marks = []
    for i in range(8):
        w = [("dres", 48)] if i == 0 else []
        P.vop("tensor_tensor", waits=w, out=a1[:, i, :], in0=a1[:, i, :], in1=a3[:, i, :], op=ALU.add)
        m = P.vop("tensor_tensor", out=a1[:, i, :], in0=a1[:, i, :], in1=a2[:, i, :], op=ALU.add)
        x1_marks.append(m)
    io["sb_x1"] = sb_r1
    io["x1_marks"] = x1_marks
    io["res_free"] = (sb_r2, sb_r3)
    return io


class _DequantEmitter:

    def __init__(self, P, io, d_wdq, sb_b4, sb_b6, sb_ident, ps_tp, extra_first=()):
        self.P = P
        self.io = io
        self.d_wdq = d_wdq
        self.b4, self.b6 = sb_b4, sb_b6
        self.ident, self.ps_tp = sb_ident, ps_tp
        self.pk = P.sbuf("dq_pk", [128, 2 * 1024], I32)
        self.sc = P.sbuf("dq_sc", [128, 2 * 64], F32)
        self.mi = P.sbuf("dq_mi", [128, 2048], I32)
        self.s8 = P.sbuf("dq_s8", [128, 2048], I32)
        self.a1 = P.sbuf("dq_a1", [128, 2048], BF16)
        self.a2 = P.sbuf("dq_a2", [128, 2048], BF16)
        self.mfb = P.sbuf("dq_mf", [128, 2048], BF16)
        self.sfb = P.sbuf("dq_sf", [128, 2048], BF16)
        self.nat = P.sbuf("dq_nat", [128, D], BF16)
        self.stg = P.sbuf("dq_stg", [128, D], BF16)
        self.pk3 = r3(self.pk[:, :], 1024)
        self.mf = self.mfb[:, :]
        self.sf = self.sfb[:, :]
        self.i = 0
        self.pk_prev = [0, 0]
        self.w_stg = None
        self.w_sc_use = 0
        self.done = {}
        self.extra_first = list(extra_first)
        self.srcs = [(io["gpk"], io["gsc"]), (io["upk"], io["usc"]), (io["dpk"], io["dsc"])]

    def free(self):
        for t in (self.pk, self.sc, self.mi, self.s8, self.a1, self.a2,
                  self.mfb, self.sfb, self.nat, self.stg):
            t.free()

    def step(self):
        P = self.P
        if self.i >= 48:
            return False
        mi_, i = divmod(self.i, 16)
        pkap, scap = self.srcs[mi_]
        scb_ = self.i % 2
        w = [("v", self.w_sc_use)] if self.w_sc_use else list(self.extra_first)
        self.w_sc = P.dma("y", out=self.sc[:, scb_ * 64:(scb_ + 1) * 64],
                          in_=scap[i * 128:(i + 1) * 128, :], waits=w, sem=f"dsc{scb_}")
        buf = self.i % 2
        w = [("v", self.pk_prev[buf])] if self.pk_prev[buf] else list(self.extra_first)
        w_ld = P.dma("y", out=self.pk3[:, buf, :], in_=pkap[i * 128:(i + 1) * 128, :],
                     waits=w, sem=f"dpk{buf}")
        P.vop("tensor_scalar", waits=[w_ld, self.w_sc], out=self.mi[:, 0:1024],
              in0=self.pk3[:, buf, :], scalar1=7, scalar2=None, op0=ALU.bitwise_and)
        P.vop("tensor_scalar", out=self.mi[:, 1024:2048], in0=self.pk3[:, buf, :],
              scalar1=4, scalar2=7, op0=ALU.logical_shift_right, op1=ALU.bitwise_and)
        P.vop("tensor_scalar", out=self.s8[:, 0:1024], in0=self.pk3[:, buf, :],
              scalar1=8, scalar2=None, op0=ALU.bitwise_and)
        self.pk_prev[buf] = P.vop(
            "tensor_scalar", out=self.s8[:, 1024:2048], in0=self.pk3[:, buf, :],
            scalar1=4, scalar2=8, op0=ALU.logical_shift_right, op1=ALU.bitwise_and)
        P.vop("tensor_copy", out=self.mf, in_=self.mi[:, :])
        m_cv = P.vop("tensor_copy", out=self.sf, in_=self.s8[:, :])
        P.sop("activation", waits=[("v", m_cv)], out=self.a1[:, :], in_=self.mf,
              func=AF.Relu, bias=self.b4[:, 0:1])
        m_a2 = P.sop("activation", out=self.a2[:, :], in_=self.mf,
                     func=AF.Relu, bias=self.b6[:, 0:1])
        P.vop("tensor_scalar", out=self.sf, in0=self.sf,
              scalar1=-0.25, scalar2=1.0, op0=ALU.mult, op1=ALU.add)
        P.vop("scalar_tensor_tensor", waits=[("s", m_a2)], out=self.a2[:, :],
              in0=self.mf, scalar=0.5, in1=self.a2[:, :], op0=ALU.mult, op1=ALU.add)
        P.vop("scalar_tensor_tensor", out=self.a2[:, :], in0=self.a1[:, :],
              scalar=0.5, in1=self.a2[:, :], op0=ALU.mult, op1=ALU.add)
        P.vop("tensor_tensor", out=self.a2[:, :], in0=self.a2[:, :], in1=self.sf,
              op=ALU.mult)
        natw = r3(self.nat[:, :], 2)
        v03 = r3(self.a2[:, :], 1024)
        for q in range(2):
            scb = bcast_free(self.sc[:, scb_ * 64:(scb_ + 1) * 64], 16)
            P.vop("tensor_tensor", out=natw[:, :, q], in0=v03[:, q, :],
                  in1=scb, op=ALU.mult)
        self.w_sc_use = P.cnt["v"]
        m_nat = P.cnt["v"]
        wtp = [("v", m_nat)]
        if self.w_stg:
            wtp.append(self.w_stg)
        emit_transpose_to(
            P, lambda g, j: self.nat[:, (g * 8 + j) * 128:(g * 8 + j + 1) * 128],
            lambda g: r3(self.stg[:, :], 128)[:, g * 8:(g + 1) * 8, :],
            2, self.ident[:, :], self.ps_tp, waits_first=wtp)
        blk = self.i
        self.w_stg = P.dma("y", out=self.d_wdq[blk * 128:(blk + 1) * 128, :],
                           in_=self.stg[:, :], waits=[("v", P.tp_guard)], sem="dst")
        P.wait_on("v", [("t", P.cnt["t"])])
        self.done[blk] = self.w_stg
        self.i += 1
        return True


def finish_l0_only(P, io, y0):
    sb_x1 = io["sb_x1"]
    w = None
    for i in range(8):
        w = P.dma("y", out=y0[i * 128:(i + 1) * 128, :], in_=r3(sb_x1[:, :], D)[:, i, :],
                  waits=[("v", io["x1_marks"][-1])] if i == 0 else (), sem="dout")
    P.wait_on("y", [w] + [(f"cc{i}", 1) for i in range(P.n_cc)])


N1 = 8192
SC_A = 1.0 / (64.0 * math.sqrt(2048.0))
SC_UP = 1.0 / math.sqrt(8192.0)
SC_OUT = 64.0


def emit_l1_prep(P, io, part="both"):
    if part in ("both", "ug", "compute"):
        d_w1ug_in = P.dram("d_w1ug_in", [D, D], BF16)
        d_w1ug = P.dram("d_w1ug", [NC * D, D], BF16)
        io.update(d_w1ug=d_w1ug, d_w1ug_in=d_w1ug_in)
        _prep_ug(P, io, d_w1ug_in, d_w1ug, ag=(part != "compute"))
    if part in ("both", "d", "compute"):
        d_w1d_in = P.dram("d_w1d_in", [256, N1], BF16)
        d_w1d = P.dram("d_w1d", [D, N1], BF16)
        io.update(d_w1d=d_w1d, d_w1d_in=d_w1d_in)
        _prep_d(P, io, d_w1d_in, d_w1d, ag=(part != "compute"))


def emit_l1_prep_ag(P, io, which):
    if which == "ug":
        io["w_ag_ug"] = P.coll("AllGather", ALU.bypass, ALL8,
                               [io["d_w1ug_in"][:, :].opt()],
                               [io["d_w1ug"][:, :].opt()], waits=[io["w_s_ug"]])
    else:
        io["w_ag_d"] = P.coll("AllGather", ALU.bypass, ALL8,
                              [io["d_w1d_in"][:, :].opt()],
                              [io["d_w1d"][:, :].opt()], waits=[io["w_s2_d"]])


def _prep_ug(P, io, d_w1ug_in, d_w1ug, ag=True):
    sb_t = P.sbuf("l1p_t", [128, 16 * D], BF16)
    w_c = P.dma("g", out=r3(sb_t[:, :], D), in_=tile_rows(io["w1ugt"][:, :]),
                sem="dl1a", waits=P.fence())
    w_s = P.dma("g", out=tile_rows(d_w1ug_in[:, :]), in_=r3(sb_t[:, :], D),
                waits=[w_c], sem="dl1a")
    io["w_s_ug"] = w_s
    if ag:
        emit_l1_prep_ag(P, io, "ug")
    sb_t.free()
    io.setdefault("prep_guard", []).extend([w_c, w_s])


def _prep_d(P, io, d_w1d_in, d_w1d, ag=True):
    sb_d = P.sbuf("l1p_d", [128, 2 * N1], F32)
    sb_db = P.sbuf("l1p_db", [128, 2 * N1], BF16)
    sb_sc = P.sbuf("l1p_sc", [128, 4096], F32)
    w_l = P.dma("y", out=r3(sb_d[:, :], N1), in_=tile_rows(io["w1d"][:, :]),
                sem="dl1c", waits=P.fence(io.get("prep_guard", [])))
    d3 = r3(sb_d[:, :], N1)
    first = [w_l]
    for lvl in range(6):
        S = 128 * (1 << lvl)
        for ti in range(2):
            tv = d3[:, ti, :].rearrange("p (blk c) -> p blk c", c=2 * S)
            a = tv[:, :, 0:S]
            b = tv[:, :, S:2 * S]
            nb = N1 // (2 * S)
            scr = r3(sb_sc[:, 0:nb * S], S)
            P.vop("tensor_tensor", waits=first, out=scr, in0=a, in1=b, op=ALU.subtract)
            first = []
            P.vop("tensor_tensor", out=a, in0=a, in1=b, op=ALU.add)
            P.vop("tensor_copy", out=b, in_=scr)
    m_bf = P.vop("tensor_copy", out=sb_db[:, :], in_=sb_d[:, :])
    w_s2 = P.dma("g", out=tile_rows(d_w1d_in[:, :]), in_=r3(sb_db[:, :], N1),
                 waits=[("v", m_bf)], sem="dl1c2")
    io["w_s2_d"] = w_s2
    if ag:
        emit_l1_prep_ag(P, io, "d")
    sb_d.free(); sb_db.free(); sb_sc.free()
    io.setdefault("prep_guard", []).extend([w_l, w_s2])


def emit_l1(P, io):
    d_xn1T_in = P.dram("d_xn1T_in", [D, TOWN], BF16)
    d_xn1T = P.dram("d_xn1T", [2 * D, TOWN], BF16)
    d_kxr = P.dram("d_kxr", [3 * D, TB], BF16)
    d_kvr = P.dram("d_kvr", [3 * 1024, TB], F32)
    d_at1_in = P.dram("d_at1_in", [TB, D], BF16)
    d_at1 = P.dram("d_at1", [TOWN, D], BF16)

    ps = io["ps"]
    ps_tp = io["ps_tp"]
    banks = io["banks"]
    sb_ident = io["sb_ident"]
    ident = sb_ident[:, :]

    sb_h128 = P.sbuf("c_h128", [128, 128], BF16)
    w_h = P.dma("y", out=sb_h128[:, :], in_=io["c_h128_in"][:, :], sem="dcl1",
                waits=P.fence(io.get("prep_guard", [])))
    sb_vec = P.sbuf("l1_vec", [128, 8 * 16], F32)
    vec3 = r3(sb_vec[:, :], 16)
    P.dma("y", out=vec3[:, 0, 0:8], in_=io["td8"][:, :], sem="dcl1")
    P.dma("y", out=vec3[:, 1, 0:8], in_=io["tf8"][:, :], sem="dcl1")
    P.dma("y", out=vec3[:, 2, :], in_=io["tmk16"][:, :], sem="dcl1")
    P.dma("y", out=vec3[:, 3, :], in_=io["tmv16"][:, :], sem="dcl1")
    P.dma("y", out=vec3[:, 4, :], in_=io["tmr16"][:, :], sem="dcl1")
    P.dma("y", out=vec3[:, 5, :], in_=io["suug16"][:, :], sem="dcl1")
    w_vec = ("dcl1", P.cnt["dcl1"])
    w_h = w_vec
    m_e1 = P.sop("activation", waits=[w_vec], out=vec3[:, 6, 0:8], in_=vec3[:, 0, 0:8], func=AF.Exp)
    m_eW = P.sop("activation", out=vec3[:, 6, 0:8], in_=vec3[:, 6, 0:8], func=AF.Exp, scale=-1.0)
    m_eU = P.sop("activation", out=vec3[:, 7, 0:8], in_=vec3[:, 1, 0:8], func=AF.Exp)
    m_sua = P.vop("tensor_scalar", waits=[w_vec], out=vec3[:, 5, :], in0=vec3[:, 5, :],
                  scalar1=SC_A, scalar2=None, op0=ALU.mult)

    sb_lnsq = P.sbuf("l1ln_sq", [128, D], F32)
    sb_lnst = P.sbuf("l1ln_st", [128, 8], F32)
    sb_eps2 = P.sbuf("l1_eps", [128, 1], F32)
    m_eps = P.op("g", "memset", ap=sb_eps2[:, :], constant=1e-5)
    ln_tmp = dict(sq=sb_lnsq, st=sb_lnst, eps=sb_eps2)

    sb_tps = P.sbuf("l1_tps", [128, 2 * 8 * 128], BF16)
    tps4 = sb_tps[:, :].rearrange("p (b j c) -> p b j c", b=2, j=8)
    tps_w = [None, None]
    sb_x1t = P.sbuf("l1_x1t", [128, D], F32)
    sb_xnt1 = P.sbuf("l1_xnt", [128, D], BF16)

    if "sb_x1" in io:
        x1src = lambda i: r3(io["sb_x1"][:, :], D)[:, i, :]
        x1waits = [("v", io["x1_marks"][-1])]
        ld = None
    else:
        x1d = io["x1"]
        ld = lambda i, w: P.dma("y", out=sb_x1t[:, :], in_=x1d[i * 128:(i + 1) * 128, :],
                                waits=w, sem="dx1t")
        x1src = lambda i: sb_x1t[:, 0:D]
        x1waits = []

    for i in range(8):
        w = list(x1waits) + ([("g", m_eps), ("s", m_eU)] if i == 0 else [])
        if ld is not None:
            prevln = [("v", P.cnt["v"]), ("s", P.cnt["s"])] if i > 0 else []
            wld = ld(i, w + prevln)
            w = [wld]
        lnm = emit_layernorm(P, lambda _: x1src(i), lambda _: sb_xnt1[:, 0:D], 1,
                             ln_tmp, marks_in=w)[0]
        for g in range(2):
            buf = (i * 2 + g) % 2
            wt = [("v", lnm)] if g == 0 else []
            if P.tp_guard:
                wt.append(("v", P.tp_guard))
            if tps_w[buf]:
                wt.append(tps_w[buf])
            P.pe_wait(wt)
            for j in range(8):
                P.pe("transpose", out=ps_tp[:, j * 128:(j + 1) * 128],
                     in_=sb_xnt1[:, (g * 8 + j) * 128:(g * 8 + j + 1) * 128],
                     identity=ident, inc=(j == 7))
            mk = P.cnt["t"]
            P.tp_guard = P.vop("tensor_copy", waits=[("t", mk)],
                               out=tps4[:, buf], in_=r3(ps_tp[:, 0:1024], 128))
            tps_w[buf] = P.dma(
                "g", out=tile_rows(d_xn1T_in[g * 1024:(g + 1) * 1024, i * 128:(i + 1) * 128]),
                in_=tps4[:, buf], waits=[("v", P.tp_guard)], sem=f"dxnt{buf}")
    if "sb_x1" in io:
        io["d_x1"] = P.dram("d_x1", [TOWN, D], F32)
        w_x1st = P.dma("g", out=tile_rows(io["d_x1"][:, :]), in_=r3(io["sb_x1"][:, :], D),
                       waits=[("v", io["x1_marks"][-1])], sem="dx1st")
        io["w_x1st"] = w_x1st
    w_agx = P.coll("AllGather", ALU.bypass, PAIRS, [d_xn1T_in[:, :].opt()],
                   [d_xn1T[:, :].opt()], waits=[tps_w[0], tps_w[1]])
    if "w_ag_ug" not in io:
        emit_l1_prep_ag(P, io, "ug")
        emit_l1_prep_ag(P, io, "d")
    sb_x1t.free(); sb_xnt1.free(); sb_tps.free()
    sb_lnsq.free(); sb_lnst.free(); sb_eps2.free()
    if "sb_x1" in io:
        io["sb_x1"].free()
        io["x1_guard"] = [io["w_x1st"]]

    sb_full = P.sbuf("l1_full", [128, 16 * TB], BF16)
    fu3 = r3(sb_full[:, :], TB)
    w_f1 = P.dma("y", out=fu3[:, :, 0:TOWN], in_=tile_rows(d_xn1T[0:D, :]),
                 waits=P.fence([w_agx] + io.get("x1_guard", [])), sem="dfu")
    w_f2 = P.dma("y", out=fu3[:, :, TOWN:TB], in_=tile_rows(d_xn1T[D:2 * D, :]),
                 waits=[], sem="dfu")
    sb_dif = P.sbuf("l1_dif", [128, TB], BF16)
    sb_kx = P.sbuf("l1_kx", [128, 2 * TB], BF16)
    kx3 = r3(sb_kx[:, :], TB)
    kx_prev = [None, None]
    w_kxr = None
    first_w = [w_f1, w_f2]
    for ht in range(16):
        m_df = P.vop("tensor_tensor", waits=first_w,
                     out=sb_dif[:, 1:TB], in0=fu3[:, ht, 1:TB], in1=fu3[:, ht, 0:TB - 1],
                     op=ALU.subtract)
        first_w = []
        for pi, slot in enumerate((2, 3, 4)):
            buf = (ht * 3 + pi) % 2
            w0 = [kx_prev[buf]] if kx_prev[buf] else []
            P.vop("scalar_tensor_tensor", waits=w0,
                  out=kx3[:, buf, 1:TB], in0=sb_dif[:, 1:TB],
                  scalar=vec3[:, slot, ht:ht + 1], in1=fu3[:, ht, 0:TB - 1],
                  op0=ALU.mult, op1=ALU.add)
            m_kx = P.vop("tensor_scalar", out=kx3[:, buf, 0:1], in0=fu3[:, ht, 0:1],
                         scalar1=vec3[:, slot, ht:ht + 1], scalar2=None, op0=ALU.mult)
            w_kxr = P.dma("y", out=d_kxr[pi * D + ht * 128:pi * D + (ht + 1) * 128, :],
                          in_=kx3[:, buf, :], waits=[("v", m_kx)], sem=f"dkxr{buf}")
            kx_prev[buf] = w_kxr
    sb_full.free(); sb_dif.free(); sb_kx.free()

    sb_k = P.sbuf("l1s_k", [128, TB], F32)
    sb_v = P.sbuf("l1s_v", [128, TB], F32)
    sb_r = P.sbuf("l1s_r", [128, TB], F32)
    sb_E = P.sbuf("l1s_E", [128, TB], F32)
    sb_F = P.sbuf("l1s_F", [128, TB], F32)
    sb_Ev = P.sbuf("l1s_Ev", [128, TB], F32)
    sb_S1 = P.sbuf("l1s_S1", [128, TB], F32)
    sb_S2 = P.sbuf("l1s_S2", [128, TB], F32)
    sb_A = P.sbuf("l1s_A", [128, TB], F32)
    sb_m = P.sbuf("l1s_m", [128, 2], F32)
    sb_rrT = P.sbuf("l1_rrT", [128, 8 * TB], BF16)
    rr3 = r3(sb_rrT[:, :], TB)
    wk_prev = []
    m_rr = 0

    def emit_scan(dt, w_pst_dt):
        nonlocal wk_prev, m_rr
        w_k = P.dma("y", out=sb_k[:, :], in_=d_kvr[dt * 128:(dt + 1) * 128, :],
                    waits=[w_pst_dt] + wk_prev, sem="dsk")
        w_v = P.dma("y", out=sb_v[:, :], in_=d_kvr[1024 + dt * 128:1024 + (dt + 1) * 128, :],
                    sem="dsk")
        w_r = P.dma("y", out=sb_r[:, :], in_=d_kvr[2048 + dt * 128:2048 + (dt + 1) * 128, :],
                    sem="dsk")
        P.vop("tensor_reduce", waits=[w_k, w_v, w_r], out=sb_m[:, 0:1], in_=sb_k[:, :],
              axis=AX.X, op=ALU.max, negate=True)
        m_E = P.sop("activation", waits=[("v", P.cnt["v"])], out=sb_E[:, :], in_=sb_k[:, :],
                    func=AF.Exp, bias=sb_m[:, 0:1])
        P.vop("tensor_scalar", waits=[("s", m_E)], out=sb_F[:, :], in0=sb_E[:, :],
              scalar1=vec3[:, 7, dt:dt + 1], scalar2=None, op0=ALU.mult)
        P.vop("tensor_tensor", out=sb_Ev[:, :], in0=sb_E[:, :], in1=sb_v[:, :], op=ALU.mult)
        P.op("v", "memset", ap=sb_S1[:, 0:1], constant=0.0)
        P.op("v", "memset", ap=sb_S2[:, 0:1], constant=0.0)
        P.vop("tensor_copy", out=sb_S1[:, 1:TB], in_=sb_Ev[:, 0:TB - 1])
        P.vop("tensor_copy", out=sb_S2[:, 1:TB], in_=sb_E[:, 0:TB - 1])
        P.op("v", "memset", ap=sb_A[:, :], constant=1.0)
        P.vop("tensor_scalar", out=sb_A[:, :], in0=sb_A[:, :],
              scalar1=vec3[:, 6, dt:dt + 1], scalar2=None, op0=ALU.mult)
        P.vop("tensor_tensor_scan", out=sb_S1[:, :], data0=sb_A[:, :], data1=sb_S1[:, :],
              initial=0.0, op0=ALU.mult, op1=ALU.add)
        P.vop("tensor_tensor_scan", out=sb_S2[:, :], data0=sb_A[:, :], data1=sb_S2[:, :],
              initial=0.0, op0=ALU.mult, op1=ALU.add)
        P.vop("tensor_tensor", out=sb_Ev[:, :], in0=sb_F[:, :], in1=sb_v[:, :], op=ALU.mult)
        P.vop("tensor_tensor", out=sb_S1[:, :], in0=sb_S1[:, :], in1=sb_Ev[:, :], op=ALU.add)
        P.vop("tensor_tensor", out=sb_S2[:, :], in0=sb_S2[:, :], in1=sb_F[:, :], op=ALU.add)
        P.vop("reciprocal", out=sb_S2[:, :], in_=sb_S2[:, :])
        P.vop("tensor_tensor", out=sb_S1[:, :], in0=sb_S1[:, :], in1=sb_S2[:, :], op=ALU.mult)
        m_rr = P.vop("tensor_tensor", out=rr3[:, dt, :], in0=sb_S1[:, :], in1=sb_r[:, :],
                     op=ALU.mult)
        wk_prev = [("v", m_rr)]

    sb_rhs1 = P.sbuf("l1_rhs", [128, 16 * TB], BF16)
    rh3 = r3(sb_rhs1[:, :], TB)
    sb_lw1 = P.sbuf("l1_lw", [128, 2 * 16 * 128], BF16)
    lw4 = sb_lw1[:, :].rearrange("p (b h j) -> p b h j", b=2, h=16)
    sb_pst = P.sbuf("l1_pst", [128, TB], F32)
    wsrc = {0: io["wk1t"], 1: io["wv1t"], 2: io["wr1t"]}
    w_pst = None
    lw_prev = [None, None]
    for pi in range(3):
        w_r = P.dma("y", out=r3(sb_rhs1[:, :], TB), in_=tile_rows(d_kxr[pi * D:(pi + 1) * D, :]),
                    waits=P.fence([kx_prev[0], kx_prev[1]]) if pi == 0 else [("t", P.cnt["t"])],
                    sem="drh1")
        for dt in range(8):
            buf = dt % 2
            wl = [lw_prev[buf]] if lw_prev[buf] else (
                P.fence([kx_prev[0], kx_prev[1]]) if pi == 0 else [])
            w_lw = P.dma("g", out=lw4[:, buf, :, :],
                         in_=tile_rows(wsrc[pi][:, dt * 128:(dt + 1) * 128]),
                         waits=wl, sem=f"dlw1{buf}")
            for tc in range(4):
                P.pe_wait([w_lw, w_r] if tc == 0 else [])
                steps = [(lw4[:, buf, ht, :], rh3[:, ht, tc * 512:(tc + 1) * 512])
                         for ht in range(16)]
                p, bi, mk = chain(P, banks, steps)
                if w_pst and tc == 0:
                    P.wait_on("v" if pi != 2 else "s", [w_pst])
                if pi == 2:
                    hcm = P.sop("activation", waits=[("t", mk)],
                                out=sb_pst[:, tc * 512:(tc + 1) * 512], in_=p[:, 0:512],
                                func=AF.Sigmoid)
                    banks.release(bi, ("s", hcm))
                else:
                    hcm = P.vop("tensor_copy", waits=[("t", mk)],
                                out=sb_pst[:, tc * 512:(tc + 1) * 512], in_=p[:, 0:512])
                    banks.release(bi, ("v", hcm))
            lw_prev[buf] = ("t", P.cnt["t"])
            w_pst = P.dma("y", out=d_kvr[pi * 1024 + dt * 128:pi * 1024 + (dt + 1) * 128, :],
                          in_=sb_pst[:, :],
                          waits=[("s" if pi == 2 else "v", hcm)], sem="dpst")
            if pi == 2:
                emit_scan(dt, w_pst)
    sb_rhs1.free(); sb_lw1.free(); sb_pst.free()
    w_pst_last = w_pst
    for t in (sb_k, sb_v, sb_r, sb_E, sb_F, sb_Ev, sb_S1, sb_S2, sb_A, sb_m):
        t.free()

    sb_wo1 = P.sbuf("l1_wo1", [128, 8 * D], BF16)
    w_wo1 = P.dma("g", out=r3(sb_wo1[:, :], D), in_=tile_rows(io["wo1"][:, :]),
                  sem="dwo1", waits=P.fence())
    wo13 = r3(sb_wo1[:, :], D)
    sb_ost = P.sbuf("l1_ost", [128, D], BF16)
    sb_xp = P.sbuf("l1f_xp", [128, 16 * TOWN], F32)
    xp3 = r3(sb_xp[:, :], TOWN)
    sb_bsc = P.sbuf("l1f_bsc", [128, TOWN], F32)
    sb_y = P.sbuf("l1f_y", [128, 16 * TOWN], BF16)
    y3 = r3(sb_y[:, :], TOWN)
    sb_xin = P.sbuf("l1f_xin", [128, 2 * TOWN], BF16)
    xin3 = r3(sb_xin[:, :], TOWN)
    xin_m = [0, 0]
    m_cast = [0] * 16
    m_h128 = 0

    def job_xp(ft):
        buf = ft % 2
        w = [w_pst_last, tps_w[0], tps_w[1]] if ft < 2 else []
        if xin_m[buf]:
            w.append(("v", xin_m[buf]))
        wld = P.dma("y", out=xin3[:, buf, :], in_=d_xn1T_in[ft * 128:(ft + 1) * 128, :],
                    waits=w, sem=f"dxi{buf}")
        xin_m[buf] = P.vop("tensor_scalar", waits=[wld], out=xp3[:, ft, :],
                           in0=xin3[:, buf, :], scalar1=vec3[:, 5, ft:ft + 1],
                           scalar2=None, op0=ALU.mult)

    def job_bfly(a, b):
        P.vop("tensor_tensor", out=sb_bsc[:, :], in0=a, in1=b, op=ALU.subtract)
        P.vop("tensor_tensor", out=a, in0=a, in1=b, op=ALU.add)
        P.vop("tensor_copy", out=b, in_=sb_bsc[:, :])

    def job_cast(ft):
        m_cast[ft] = P.vop("tensor_copy", out=y3[:, ft, :], in_=xp3[:, ft, :])

    def job_h128(ft, tc):
        nonlocal m_h128
        P.pe_wait(([("v", m_cast[ft])] if tc == 0 else []) +
                  ([w_h] if (ft == 0 and tc == 0) else []))
        bi = banks.acquire()
        p = banks.ps[bi]
        P.pe("matmul", out=p[:, 0:512], lhsT=sb_h128[:, :],
             rhs=y3[:, ft, tc * 512:(tc + 1) * 512], start=True, stop=True, inc=True)
        m_h128 = P.vop("tensor_copy", waits=[("t", P.cnt["t"])],
                       out=y3[:, ft, tc * 512:(tc + 1) * 512], in_=p[:, 0:512])
        banks.release(bi, ("v", m_h128))

    vec_jobs = [(lambda ft=ft: job_xp(ft)) for ft in range(16)]
    for lvl in range(4):
        S = 1 << lvl
        for base in range(0, 16, 2 * S):
            for k in range(S):
                vec_jobs.append(lambda a=xp3[:, base + k, :],
                                b=xp3[:, base + k + S, :]: job_bfly(a, b))
    vec_jobs.extend((lambda ft=ft: job_cast(ft)) for ft in range(16))
    h_jobs = [(ft, tc) for ft in range(16) for tc in range(2)]
    vj = hj = 0

    P.pe_wait([w_wo1])
    w_at1 = None
    for tt in range(16):
        if w_at1:
            P.wait_on("v", [w_at1])
        hcm = 0
        for hc in range(4):
            steps = [(rr3[:, dt, tt * 128:(tt + 1) * 128],
                      wo13[:, dt, hc * 512:(hc + 1) * 512]) for dt in range(8)]
            p, bi, mk = chain(P, banks, steps)
            hcm = P.vop("tensor_copy", waits=[("t", mk)],
                        out=sb_ost[:, hc * 512:(hc + 1) * 512], in_=p[:, 0:512])
            banks.release(bi, ("v", hcm))
        w_at1 = P.dma("y", out=d_at1_in[tt * 128:(tt + 1) * 128, :], in_=sb_ost[:, :],
                      waits=[("v", hcm)], sem="dat1")
        for _ in range(6):
            if vj < len(vec_jobs):
                vec_jobs[vj]()
                vj += 1
        if vj >= len(vec_jobs):
            for _ in range(4):
                if hj < len(h_jobs):
                    job_h128(*h_jobs[hj])
                    hj += 1
    w_rs_at1 = P.coll("ReduceScatter", ALU.add, PAIRS, [d_at1_in[:, :].opt()],
                      [d_at1[:, :].opt()], waits=[w_at1])
    while vj < len(vec_jobs):
        vec_jobs[vj]()
        vj += 1
    while hj < len(h_jobs):
        job_h128(*h_jobs[hj])
        hj += 1
    m_h_last = m_h128
    sb_wo1.free(); sb_ost.free(); sb_rrT.free()
    sb_xp.free(); sb_bsc.free(); sb_xin.free()
    ys3 = y3

    sb_wsc = P.sbuf("l1f_wsc", [128, 128], F32)
    sb_sdp = P.sbuf("l1f_sdp", [128, 64], F32)
    w_w1 = P.dma("y", out=sb_wsc[:, :], in_=io["wscug128"][:, :], sem="dcl2",
                 waits=P.fence([w_at1, w_pst_last]))
    P.dma("y", out=sb_sdp[:, :], in_=io["sudp64"][:, :], sem="dcl2")
    w_w2 = None

    sb_of = P.sbuf("l1f_onesf", [1, 128], F32)
    sb_dprow = P.sbuf("l1f_dprow", [1, D], F32)
    sb_osc = P.sbuf("l1f_osc", [128, D], F32)
    m_o1 = P.op("g", "memset", waits=P.fence([w_at1, w_pst_last]), ap=sb_of[:, :],
                constant=1.0)
    P.dma("y", out=sb_dprow[:, :], in_=io["wscdp_row"][:, :], sem="dcl2")
    w_cl2 = ("dcl2", P.cnt["dcl2"])
    P.vop("tensor_scalar", waits=[w_cl2], out=sb_wsc[:, 64:128], in0=sb_wsc[:, 64:128],
          scalar1=64.0, scalar2=None, op0=ALU.mult)
    P.vop("tensor_tensor", out=sb_sdp[:, :], in0=sb_sdp[:, :], in1=sb_wsc[:, 0:64],
          op=ALU.mult)
    m_scl = P.vop("tensor_scalar", out=sb_sdp[:, :], in0=sb_sdp[:, :],
                  scalar1=SC_UP, scalar2=None, op0=ALU.mult)
    P.pe_wait([("g", m_o1), w_cl2])
    for hc in range(4):
        bi = banks.acquire()
        p = banks.ps[bi]
        P.pe("matmul", out=p[:, 0:512], lhsT=sb_of[0:1, :],
             rhs=sb_dprow[0:1, hc * 512:(hc + 1) * 512], start=True, stop=True, inc=True)
        m_osc = P.vop("tensor_scalar", waits=[("t", P.cnt["t"])],
                      out=sb_osc[:, hc * 512:(hc + 1) * 512], in0=p[:, 0:512],
                      scalar1=SC_OUT, scalar2=None, op0=ALU.mult)
        banks.release(bi, ("v", m_osc))

    sb_acc = P.sbuf("l1f_acc", [128, 8 * D], F32)
    ac3 = r3(sb_acc[:, :], D)
    sb_lwu = P.sbuf("l1f_lwu", [128, 4 * 16 * 128], BF16)
    lwu4 = sb_lwu[:, :].rearrange("p (b h j) -> p b h j", b=4, h=16)
    sb_y2 = P.sbuf("l1f_y2", [128, 8 * 512], BF16)
    y23 = r3(sb_y2[:, :], 512)
    sb_wdt = P.sbuf("l1f_wdt", [128, 8 * D], BF16)
    wdt3 = r3(sb_wdt[:, :], D)
    sb_sil = P.sbuf("l1f_sil", [128, 512], F32)
    lwu_prev = [None] * 4
    wdt_prev = None
    m_sil_prev = 0
    pend = None

    def h128_flush():
        nonlocal pend
        if pend is None:
            return
        jp, m_gpp = pend
        P.pe_wait([("v", m_gpp)])
        bi = banks.acquire()
        p = banks.ps[bi]
        P.pe("matmul", out=p[:, 0:512], lhsT=sb_h128[:, :],
             rhs=sb_y2[:, jp * 512:(jp + 1) * 512], start=True, stop=True, inc=True)
        m_y2 = P.vop("tensor_copy", waits=[("t", P.cnt["t"])],
                     out=sb_y2[:, jp * 512:(jp + 1) * 512], in_=p[:, 0:512])
        banks.release(bi, ("v", m_y2))
        pend = None

    for th2 in range(2):
        toff = th2 * 512
        for sup in range(8):
            w_wdt = None
            for j in range(8):
                f2 = sup * 8 + j
                wd_w = [wdt_prev] if (wdt_prev and j == 0) else []
                if sup == 0 and th2 == 0 and j == 0:
                    wd_w = P.fence([io["w_ag_d"], w_at1, w_pst_last])
                w_wdt = P.dma("y", out=wdt3[:, j, :], in_=io["d_w1d"][:, f2 * 128:(f2 + 1) * 128],
                              waits=wd_w, sem="dwdt", tp=True)
            for j in range(8):
                f2 = sup * 8 + j
                nu, ng = f2, 64 + f2
                bu_, bg_ = (2 * j) % 4, (2 * j + 1) % 4
                wlu = [lwu_prev[bu_]] if lwu_prev[bu_] else []
                if sup == 0 and th2 == 0 and j == 0:
                    wlu = P.fence([io["w_ag_ug"], w_at1, w_pst_last])
                elif sup == 0 and th2 == 0 and j == 1:
                    wlu = wlu + [w_at1, w_pst_last]
                w_lu = P.dma("g", out=lwu4[:, bu_, :, :],
                             in_=tile_rows(io["d_w1ug"][(nu // 16) * D:(nu // 16 + 1) * D,
                                                        (nu % 16) * 128:(nu % 16 + 1) * 128]),
                             waits=wlu, sem=f"dlu{bu_}")
                wlg = [lwu_prev[bg_]] if lwu_prev[bg_] else (
                    [w_at1, w_pst_last] if (sup == 0 and th2 == 0 and j < 2) else [])
                w_lg = P.dma("g", out=lwu4[:, bg_, :, :],
                             in_=tile_rows(io["d_w1ug"][(ng // 16) * D:(ng // 16 + 1) * D,
                                                        (ng % 16) * 128:(ng % 16 + 1) * 128]),
                             waits=wlg, sem=f"dlu{bg_}")
                P.pe_wait([w_lu, w_lg] + ([("v", m_scl), ("v", m_h_last)]
                                          if (sup == 0 and th2 == 0 and j == 0) else []))
                stepsu = [(lwu4[:, bu_, ft, :], ys3[:, ft, toff:toff + 512]) for ft in range(16)]
                pu, biu, mku = chain(P, banks, stepsu)
                stepsg = [(lwu4[:, bg_, ft, :], ys3[:, ft, toff:toff + 512]) for ft in range(16)]
                pg, big, mkg = chain(P, banks, stepsg)
                lwu_prev[bu_] = ("t", P.cnt["t"])
                lwu_prev[bg_] = ("t", P.cnt["t"])
                h128_flush()
                m_sil = P.sop("activation", waits=[("t", mkg), ("v", m_sil_prev)],
                              out=sb_sil[:, :], in_=pg[:, 0:512],
                              func=AF.Silu, scale=sb_wsc[:, 64 + f2:65 + f2])
                m_gg = P.vop("tensor_tensor", waits=[("s", m_sil), ("t", mku)],
                             out=sb_sil[:, :],
                             in0=sb_sil[:, :], in1=pu[:, 0:512], op=ALU.mult)
                m_gp = P.vop("tensor_scalar", out=sb_y2[:, j * 512:(j + 1) * 512],
                             in0=sb_sil[:, :], scalar1=sb_sdp[:, f2:f2 + 1], scalar2=None,
                             op0=ALU.mult)
                banks.release(big, ("s", m_sil))
                banks.release(biu, ("v", m_gg))
                m_sil_prev = m_gp
                pend = (j, m_gp)
            h128_flush()
            for tt4 in range(4):
                for hc in range(4):
                    P.pe_wait([w_wdt, ("v", P.cnt["v"])] if (tt4 == 0 and hc == 0) else [])
                    steps = [(y23[:, j, tt4 * 128:(tt4 + 1) * 128],
                              wdt3[:, j, hc * 512:(hc + 1) * 512]) for j in range(8)]
                    p, bi, mk = chain(P, banks, steps)
                    dst = ac3[:, th2 * 4 + tt4, hc * 512:(hc + 1) * 512]
                    if sup == 0:
                        m_ac = P.vop("tensor_copy", waits=[("t", mk)], out=dst, in_=p[:, 0:512])
                    else:
                        m_ac = P.vop("tensor_tensor", waits=[("t", mk)], out=dst,
                                     in0=dst, in1=p[:, 0:512], op=ALU.add)
                    banks.release(bi, ("v", m_ac))
            wdt_prev = ("t", P.cnt["t"])
    m_ffn = P.cnt["v"]
    sb_y.free(); sb_lwu.free(); sb_y2.free(); sb_wdt.free(); sb_sil.free()

    y1 = io["y1"]
    sb_fx = P.sbuf("l1r_x", [128, D], F32)
    sb_fa = P.sbuf("l1r_a", [128, D], BF16)
    w_out = None
    for i in range(8):
        wx = []
        if i == 0:
            wx = P.fence([w_rs_at1])
        x1src_d = io["x1"] if "x1" in io else io["d_x1"]
        wx1 = P.dma("y", out=sb_fx[:, :], in_=x1src_d[i * 128:(i + 1) * 128, :],
                    waits=wx + ([w_out] if w_out else []), sem="dfx")
        wa1 = P.dma("y", out=sb_fa[:, :], in_=d_at1[i * 128:(i + 1) * 128, :],
                    waits=[], sem="dfx")
        P.vop("tensor_tensor", waits=[wx1, wa1, ("v", m_ffn)],
              out=ac3[:, i, :], in0=ac3[:, i, :], in1=sb_osc[:, :], op=ALU.mult)
        P.vop("tensor_tensor", out=ac3[:, i, :], in0=ac3[:, i, :], in1=sb_fx[:, :], op=ALU.add)
        m_f = P.vop("tensor_tensor", out=ac3[:, i, :], in0=ac3[:, i, :], in1=sb_fa[:, :],
                    op=ALU.add)
        w_out = P.dma("y", out=y1[i * 128:(i + 1) * 128, :], in_=ac3[:, i, :],
                      waits=[("v", m_f)], sem="dy1")
    P.wait_on("y", [w_out] + [(f"cc{i}", 1) for i in range(P.n_cc)])


def hadamard(n):
    h = np.array([[1.0]], np.float32)
    while h.shape[0] < n:
        h = np.block([[h, h], [h, -h]])
    return h


def make_consts():
    c = {}
    c["c_ident_in"] = np.eye(128).astype(BF16NP)
    mask = np.where(np.arange(128)[:, None] > np.arange(128)[None, :], MASK_NEG, 0.0)
    c["c_mask_in"] = mask.astype(np.float32)
    c["c_ones_in"] = np.ones((128, 128), BF16NP)
    freqs = 1.0 / (10000.0 ** (np.arange(0, HD, 2, dtype=np.float64) / HD))
    ang = np.arange(TB, dtype=np.float64)[:, None] * freqs[None, :]
    c["rope_cos"] = np.cos(ang).astype(np.float32)
    c["rope_sin"] = np.sin(ang).astype(np.float32)
    c["c_h128_in"] = hadamard(128).astype(BF16NP)
    return c


def make_l0_inputs(full, consts):
    x = np.asarray(full["L0_attn_x"])
    wqkv = np.asarray(full["L0_attn_Wqkv"])
    wo = np.asarray(full["L0_attn_Wo"])
    gpk = np.asarray(full["L0_ffn_gate_W_packed"])
    gsc = np.asarray(full["L0_ffn_gate_scales"])
    upk = np.asarray(full["L0_ffn_up_W_packed"])
    usc = np.asarray(full["L0_ffn_up_scales"])
    dpk = np.asarray(full["L0_ffn_down_W_packed"])
    dsc = np.asarray(full["L0_ffn_down_scales"])
    cores = []
    for c in range(NC):
        b, h = c // 2, c % 2
        d = dict(consts)
        d["xb"] = np.ascontiguousarray(x[b])
        d["xo"] = np.ascontiguousarray(x[b, h * TOWN:(h + 1) * TOWN])
        d["wq"] = np.ascontiguousarray(wqkv[:, h * 1024:(h + 1) * 1024])
        d["wk"] = np.ascontiguousarray(wqkv[:, 2048 + 256 * h:2048 + 256 * h + 256])
        d["wv"] = np.ascontiguousarray(wqkv[:, 2560 + 256 * h:2560 + 256 * h + 256])
        d["wo0"] = np.ascontiguousarray(wo[h * 1024:(h + 1) * 1024, :])
        d["gpk"] = np.ascontiguousarray(gpk[c * 2048:(c + 1) * 2048, :])
        d["gsc"] = np.ascontiguousarray(gsc[c * 2048:(c + 1) * 2048, :])
        d["upk"] = np.ascontiguousarray(upk[c * 2048:(c + 1) * 2048, :])
        d["usc"] = np.ascontiguousarray(usc[c * 2048:(c + 1) * 2048, :])
        d["dpk"] = np.ascontiguousarray(dpk[:, c * 1024:(c + 1) * 1024])
        d["dsc"] = np.ascontiguousarray(dsc[:, c * 64:(c + 1) * 64])
        cores.append(d)
    return cores


L0_IN_SPECS = [
    ("xb", [TB, D], F32), ("xo", [TOWN, D], F32),
    ("wq", [D, 1024], F32), ("wk", [D, 256], F32), ("wv", [D, 256], F32),
    ("wo0", [1024, D], F32),
    ("gpk", [2048, 1024], I32), ("gsc", [2048, 64], F32),
    ("upk", [2048, 1024], I32), ("usc", [2048, 64], F32),
    ("dpk", [2048, 1024], I32), ("dsc", [2048, 64], F32),
    ("c_ident_in", [128, 128], BF16), ("c_mask_in", [128, 128], F32),
    ("c_ones_in", [128, 128], BF16),
    ("rope_cos", [TB, 64], F32), ("rope_sin", [TB, 64], F32),
    ("c_h128_in", [128, 128], BF16),
]


def build_l0(stage=99):
    P = Prog()
    io = {"stage": stage}
    for name, shape, dt in L0_IN_SPECS:
        io[name] = P.dram(name, shape, dt, kind="ExternalInput")
    y0 = P.dram("y0", [TOWN, D], F32, kind="ExternalOutput")
    io["y0"] = y0
    emit_l0(P, io)
    if stage >= 99:
        finish_l0_only(P, io, y0)
    nc = P.build()
    return nc, P


def l0_forward(full_inputs, stage=99):
    consts = make_consts()
    cores = make_l0_inputs(full_inputs, consts)
    nc, P = build_l0(stage)
    res = run_bass_kernel_spmd(nc, cores, core_ids=list(range(NC)))
    out = np.zeros((4, TB, D), np.float32)
    for c in range(NC):
        b, h = c // 2, c % 2
        out[b, h * TOWN:(h + 1) * TOWN] = np.asarray(res.results[c]["y0"])
    return out




L1_IN_SPECS = [
    ("x1", [TOWN, D], F32),
    ("wk1t", [D, 1024], F32), ("wv1t", [D, 1024], F32), ("wr1t", [D, 1024], F32),
    ("wo1", [1024, D], F32),
    ("td8", [128, 8], F32), ("tf8", [128, 8], F32),
    ("tmk16", [128, 16], F32), ("tmv16", [128, 16], F32), ("tmr16", [128, 16], F32),
    ("w1ugt", [D, D], F32),
    ("w1d", [256, N1], F32),
    ("suug16", [128, 16], F32), ("sudp64", [128, 64], F32),
    ("wscug128", [128, 128], F32), ("wscdp_row", [1, D], F32),
    ("c_ident_in", [128, 128], BF16), ("c_h128_in", [128, 128], BF16),
]


def make_l1_inputs(full, consts):
    Wk = np.asarray(full["L1_attn_Wk"])
    Wv = np.asarray(full["L1_attn_Wv"])
    Wr = np.asarray(full["L1_attn_Wr"])
    Wo = np.asarray(full["L1_attn_Wo"])
    td = np.asarray(full["L1_attn_time_decay"]).reshape(-1)
    tf = np.asarray(full["L1_attn_time_first"]).reshape(-1)
    tmk = np.asarray(full["L1_attn_time_mix_key"]).reshape(-1)
    tmv = np.asarray(full["L1_attn_time_mix_value"]).reshape(-1)
    tmr = np.asarray(full["L1_attn_time_mix_receptance"]).reshape(-1)
    W_up = np.asarray(full["L1_ffn_W_up"])
    W_gate = np.asarray(full["L1_ffn_W_gate"])
    W_down = np.asarray(full["L1_ffn_W_down"])
    su_ug = np.asarray(full["L1_ffn_SU_ug"]).reshape(-1)
    su_dp = np.asarray(full["L1_ffn_SU_dp"]).reshape(-1)
    wsc_ug = np.asarray(full["L1_ffn_Wscale_ug"]).reshape(-1)
    wsc_dp = np.asarray(full["L1_ffn_Wscale_dp"]).reshape(-1)
    W_ug = np.concatenate([W_up, W_gate], 0)
    cores = []
    for c in range(NC):
        b, h = c // 2, c % 2
        dmy = slice(1024 * h, 1024 * (h + 1))
        d = {}
        d["wk1t"] = np.ascontiguousarray(Wk[dmy].T)
        d["wv1t"] = np.ascontiguousarray(Wv[dmy].T)
        d["wr1t"] = np.ascontiguousarray(Wr[dmy].T)
        d["wo1"] = np.ascontiguousarray(Wo[:, dmy].T)
        d["td8"] = np.ascontiguousarray(td[dmy].reshape(8, 128).T)
        d["tf8"] = np.ascontiguousarray(tf[dmy].reshape(8, 128).T)
        d["tmk16"] = np.ascontiguousarray(tmk.reshape(16, 128).T)
        d["tmv16"] = np.ascontiguousarray(tmv.reshape(16, 128).T)
        d["tmr16"] = np.ascontiguousarray(tmr.reshape(16, 128).T)
        d["w1ugt"] = np.ascontiguousarray(W_ug[c * 2048:(c + 1) * 2048, :].T)
        d["w1d"] = np.ascontiguousarray(W_down[c * 256:(c + 1) * 256, :])
        d["suug16"] = np.ascontiguousarray(su_ug.reshape(16, 128).T)
        d["sudp64"] = np.ascontiguousarray(su_dp.reshape(64, 128).T)
        d["wscug128"] = np.ascontiguousarray(wsc_ug.reshape(128, 128).T)
        d["wscdp_row"] = np.ascontiguousarray(wsc_dp.reshape(1, D))
        d["c_ident_in"] = consts["c_ident_in"]
        d["c_h128_in"] = consts["c_h128_in"]
        cores.append(d)
    return cores


def build_l1(stage=99):
    P = Prog()
    io = {"stage": stage}
    for name, shape, dt in L1_IN_SPECS:
        io[name] = P.dram(name, shape, dt, kind="ExternalInput")
    y1 = P.dram("y1", [TOWN, D], F32, kind="ExternalOutput")
    io["y1"] = y1
    ps = [P.psum(f"ps{i}", [128, 512], F32) for i in range(7)]
    ps_tp = P.psum("ps_tp", [128, 1024], BF16)
    sb_ident = P.sbuf("c_ident", [128, 128], BF16)
    P.dma("y", out=sb_ident[:, :], in_=io["c_ident_in"][:, :], sem="dcst")
    io.update(ps=ps, ps_tp=ps_tp, banks=Banks(P, ps[0:5]), sb_ident=sb_ident)
    P.wait_on("t", [("dcst", 16)])
    emit_l1_prep(P, io, part="both")
    emit_l1(P, io)
    nc = P.build()
    return nc, P


def l1_forward(x1_full, l1_cores, stage=99):
    nc, P = build_l1(stage)
    for c in range(NC):
        b, h = c // 2, c % 2
        l1_cores[c]["x1"] = np.ascontiguousarray(x1_full[b, h * TOWN:(h + 1) * TOWN])
    res = run_bass_kernel_spmd(nc, l1_cores, core_ids=list(range(NC)))
    out = np.zeros((4, TB, D), np.float32)
    for c in range(NC):
        b, h = c // 2, c % 2
        out[b, h * TOWN:(h + 1) * TOWN] = np.asarray(res.results[c]["y1"])
    return out


FUSED_IN_SPECS = L0_IN_SPECS + [s for s in L1_IN_SPECS
                                if s[0] not in ("x1", "c_ident_in", "c_h128_in")]


def build_fused():
    P = Prog()
    io = {}
    for name, shape, dt in FUSED_IN_SPECS:
        io[name] = P.dram(name, shape, dt, kind="ExternalInput")
    y1 = P.dram("y1", [TOWN, D], F32, kind="ExternalOutput")
    io["y1"] = y1
    io["early_prep"] = True
    emit_l0(P, io)
    for t in io["res_free"]:
        t.free()
    emit_l1(P, io)
    nc = P.build()
    return nc, P


def kernel(**inputs):
    consts = make_consts()
    c0 = make_l0_inputs(inputs, consts)
    c1 = make_l1_inputs(inputs, consts)
    cores = []
    for c in range(NC):
        d = dict(c0[c])
        d.update({k: v for k, v in c1[c].items() if k != "x1"})
        cores.append(d)
    nc, P = build_fused()
    res = run_bass_kernel_spmd(nc, cores, core_ids=list(range(NC)))
    out = np.zeros((4, TB, D), np.float32)
    for c in range(NC):
        b, h = c // 2, c % 2
        out[b, h * TOWN:(h + 1) * TOWN] = np.asarray(res.results[c]["y1"])
    return out


def kernel_2launch(**inputs):
    consts = make_consts()
    c0 = make_l0_inputs(inputs, consts)
    nc0, _ = build_l0()
    res0 = run_bass_kernel_spmd(nc0, c0, core_ids=list(range(NC)))
    c1 = make_l1_inputs(inputs, consts)
    for c in range(NC):
        c1[c]["x1"] = np.asarray(res0.results[c]["y0"])
    nc1, _ = build_l1()
    res1 = run_bass_kernel_spmd(nc1, c1, core_ids=list(range(NC)))
    out = np.zeros((4, TB, D), np.float32)
    for c in range(NC):
        b, h = c // 2, c % 2
        out[b, h * TOWN:(h + 1) * TOWN] = np.asarray(res1.results[c]["y1"])
    return out
